# version 14
# baseline (speedup 1.0000x reference)
import numpy as np
import concourse.bass as bass
import concourse.mybir as mybir
from concourse.bass_utils import run_bass_kernel_spmd

F32 = mybir.dt.float32
BF16 = mybir.dt.bfloat16
AF = mybir.ActivationFunctionType
ALU = mybir.AluOpType
AX = mybir.AxisListType

S, D, H, HD, WIN = 4096, 1024, 16, 64, 512
TILE, NT, SC = 1024, 4, 512
NCH = 8
E8, RR, FF = 8, 256, 4096
SCALE = 0.125
CEXP = -8.0

ENGINES = ("tensor", "vector", "scalar", "sync", "gpsimd")


class Reg:
    __slots__ = ("writers", "readers")

    def __init__(self):
        self.writers = {}
        self.readers = {}

    def wrote(self, gate):
        s, v = gate
        self.writers = {s: v}
        self.readers = {}

    def read(self, gate):
        s, v = gate
        if self.readers.get(s, 0) < v:
            self.readers[s] = v

def alias_edge(dsts, srcs):
    merged = {}
    for o in srcs:
        for d in (o.writers, o.readers):
            for s, v in d.items():
                if merged.get(s, 0) < v:
                    merged[s] = v
    for o in srcs:
        o.writers = {}
        o.readers = {}
    for r in dsts:
        r.writers = {}
        r.readers = dict(merged)


class Prog:
    def __init__(self):
        self.q = {e: [] for e in ENGINES}
        self.cnt = {}
        self.guar = {e: {} for e in ENGINES}
        self.pend = {e: [] for e in ENGINES}

    def sem(self, name):
        self.cnt[name] = 0
        return name

    def _wait(self, eng, sname, val):
        if val <= 0:
            return
        if self.guar[eng].get(sname, 0) >= val:
            return
        self.guar[eng][sname] = val
        self.q[eng].append(("w", sname, val))

    def emit(self, eng, fn, reads=(), writes=(), inc=None, k=1):
        need = {}
        for r in reads:
            for s, v in r.writers.items():
                if need.get(s, 0) < v:
                    need[s] = v
        for r in writes:
            for d in (r.writers, r.readers):
                for s, v in d.items():
                    if need.get(s, 0) < v:
                        need[s] = v
        if inc is not None and inc[0] == "d":
            need.pop(inc, None)
        for s, v in need.items():
            self._wait(eng, s, v)
        if inc is None:
            self.q[eng].append(("op", fn, None, 0))
            self.pend[eng].append((tuple(reads), tuple(writes)))
            return None
        self.cnt[inc] += k
        gate = (inc, self.cnt[inc])
        if eng == "tensor" and inc == "T":
            self.guar[eng][inc] = self.cnt[inc]
        self.q[eng].append(("op", fn, inc, k))
        for rr, ww in self.pend[eng]:
            for r in rr:
                r.read(gate)
            for w in ww:
                w.wrote(gate)
        self.pend[eng].clear()
        for r in reads:
            r.read(gate)
        for r in writes:
            r.wrote(gate)
        return gate

    def raw_wait(self, eng, gatedict):
        for s, v in gatedict.items():
            self._wait(eng, s, v)


def build_nc():
    nc = bass.Bass("TRN2", target_bir_lowering=False, debug=False)
    P = Prog()
    for s in ("T", "V", "S", "G"):
        P.sem(s)
    DMA_SEMS = ("dxy", "dsl0", "dsl1", "dpf0", "dpf1", "dgw1", "doutw",
                "dw2a", "dw2b", "dl2a", "dl2b", "dmisc", "dbias",
                "dout0", "dout1", "dsb", "dsw2", "dsw2b", "dsl2s", "dsl2t")
    for s in DMA_SEMS:
        P.sem(s)

    dt_in = lambda n, sh, dt=F32: nc.dram_tensor(n, sh, dt, kind="ExternalInput").ap()
    x_tm = dt_in("x_tm", [S, D])
    x_fm = dt_in("x_fm", [D, S])
    pos_tm = dt_in("pos_tm", [S, D])
    pos_fm = dt_in("pos_fm", [D, S])
    gmem = dt_in("gmem", [64, D])
    convw = dt_in("convw", [4097, D])
    gate_w = dt_in("gate_w", [2048, D])
    out_w = dt_in("out_w", [D, D])
    w1f = dt_in("w1f", [D, 2048])
    w2f = dt_in("w2f", [2048, FF])
    lin2_w = dt_in("lin2_w", [FF, D])
    gate_b = dt_in("gate_b", [128, 8])
    out_b = dt_in("out_b", [128, 8])
    lin2_b = dt_in("lin2_b", [128, 8])
    router_w = dt_in("router_w", [128, 64])
    router_b = dt_in("router_b", [8, 1])
    out_fm = nc.dram_tensor("out_fm", [D, S], F32, kind="ExternalOutput").ap()
    dt_sc = lambda n, sh: nc.dram_tensor(n, sh, BF16, kind="Internal").ap()
    gw_bf = dt_sc("gw_bf", [2048, D])
    ow_bf = dt_sc("ow_bf", [D, D])
    w1_bf = dt_sc("w1_bf", [D, 2048])
    w2_bf = dt_sc("w2_bf", [2048, FF])
    l2_bf = dt_sc("l2_bf", [FF, D])

    import contextlib
    stack = contextlib.ExitStack()
    sb = lambda n, sh, dt: stack.enter_context(nc.sbuf_tensor(n, sh, dt)).ap()
    ps = lambda n, sh, dt: stack.enter_context(nc.psum_tensor(n, sh, dt)).ap()

    xy = sb("xy", [128, 8192], BF16)
    ynbf = sb("ynbf", [128, 8192], BF16)
    xpef = sb("xpef", [128, 4096], BF16)
    xpet = sb("xpet", [128, 4096], BF16)
    ah = sb("ah", [128, 16384], BF16)
    tbuf = sb("tbuf", [128, 8192], BF16)
    gw1a = sb("gw1a", [128, 16384], BF16)
    outwa = sb("outwa", [128, 8192], BF16)
    w2a = sb("w2a", [128, 16384], BF16)
    gt_tm = sb("gt_tm", [128, 1024], BF16)

    gt_fm = sb("gt_fm", [128, 1024], BF16)
    ebf = sb("ebf", [128, 512], F32)
    cfm = sb("cfm", [128, 512], BF16)
    ctm = sb("ctm", [128, 32], F32)
    csb = sb("csb", [128, 512], BF16)
    mwsb = sb("mwsb", [128, 512], BF16)
    l2a = sb("l2a", [128, 4096], BF16)
    ztm = sb("ztm", [128, 32], F32)
    rztm = sb("rztm", [128, 32], BF16)
    scr4 = sb("scr4", [128, 16], F32)
    scrf = sb("scrf", [128, 640], F32)
    outst = sb("outst", [128, 512], F32)
    statv = sb("statv", [1, 1536], F32)
    ones_b = sb("ones_b", [128, 128], BF16)
    ones_f = sb("ones_f", [128, 128], F32)
    id_f = sb("id_f", [128, 128], F32)
    id_b = sb("id_b", [128, 128], BF16)
    basis = sb("basis", [8, 1024], BF16)
    neg8 = sb("neg8", [128, 1], F32)
    epsc = sb("epsc", [128, 1], F32)
    gb_s = sb("gb_s", [128, 8], F32)
    ob_s = sb("ob_s", [128, 8], F32)
    lb_s = sb("lb_s", [128, 8], F32)
    rb_s = sb("rb_s", [8, 1], F32)
    rw_s = sb("rw_s", [128, 64], BF16)
    iot = sb("iot", [128, 1], F32)

    cbrow = l2a[0:1, 0:1024]
    pb = [ps(f"pb{i}", [128, 512], F32) for i in range(8)]
    pbb = [p.bitcast(BF16) for p in pb]

    LOC, GLO, MIX, PL0, PG0, RZ0 = 0, 4096, 8192, 12288, 14336, 15360

    R = lambda: Reg()
    Rxy, Rynbf, Rxpef, Rxpet = R(), R(), R(), R()
    Rloc, Rglo, Rmix = R(), R(), R()
    Rpl = [R() for _ in range(4)]
    Rpg = [R() for _ in range(2)]
    Rrz = [R() for _ in range(2)]
    Rah = R()
    Rtb = R()
    Rslab = [R() for _ in range(6)]
    Rgw1, Routw = R(), R()
    Rw2 = [R() for _ in range(2)]
    Rconv = [R() for _ in range(4)]
    Rl2 = [R() for _ in range(4)]
    Rout = [R() for _ in range(2)]
    Rgt_tm, Rgt_fm = R(), R()
    Rgb, Rob, Rlb, Rrb, Rrw, Rcbrow = R(), R(), R(), R(), R(), R()
    Rsgw, Rsow, Rsw1, Rsw2, Rsl2 = R(), R(), R(), R(), R()
    Rebf, Rcfm, Rctm, Rscr4, Rscrf, Rstat = R(), R(), R(), R(), R(), R()
    Rzt = [R(), R()]
    Rrzt = [R(), R()]
    Rcsb = R()
    Rmwsb = [R(), R()]
    Rb = [R() for _ in range(8)]
    Rconst = R()
    Rbias = R()


    def gp(fn, **kw):
        return P.emit("gpsimd", fn, **kw)

    def vx(fn, **kw):
        return P.emit("vector", fn, **kw)

    def sx(fn, **kw):
        return P.emit("scalar", fn, **kw)

    def tx(fn, **kw):
        return P.emit("tensor", fn, **kw)

    def dmaw(sem, fn, **kw):
        return P.emit("gpsimd", fn, inc=sem, k=16, **kw)

    def dmax(sem, fn, **kw):
        return P.emit("sync", fn, inc=sem, k=16, **kw)

    gp(lambda E: E.memset(ones_b[:], 1.0), writes=[Rconst], inc="G")
    gp(lambda E: E.memset(ones_f[:], 1.0), writes=[Rconst], inc="G")
    gp(lambda E: E.memset(neg8[:], CEXP), writes=[Rconst], inc="G")
    gp(lambda E: E.memset(epsc[:], 1e-5), writes=[Rconst], inc="G")
    gp(lambda E: E.iota(id_f[:], pattern=[[1, 128]], channel_multiplier=0,
                        allow_small_or_imprecise_dtypes=True),
       writes=[Rconst], inc="G")
    gp(lambda E: E.iota(iot[:], pattern=[[0, 1]], channel_multiplier=1,
                        allow_small_or_imprecise_dtypes=True),
       writes=[Rconst], inc="G")
    vx(lambda E: E.tensor_scalar(id_f[:], id_f[:], iot[:], None, ALU.is_equal),
       reads=[Rconst], writes=[Rconst], inc="V")
    vx(lambda E: E.tensor_copy(id_b[:], id_f[:]), reads=[Rconst], writes=[Rconst], inc="V")
    gp(lambda E: E.iota(basis[:], pattern=[[1, 8], [0, 128]], channel_multiplier=0,
                        allow_small_or_imprecise_dtypes=True),
       writes=[Rconst], inc="G")
    vx(lambda E: E.tensor_scalar(basis[:], basis[:], iot[0:8, :], None, ALU.is_equal),
       reads=[Rconst], writes=[Rconst], inc="V")
    dmax("dbias", lambda E: E.dma_start(out=gb_s[:], in_=gate_b), writes=[Rgb])
    dmax("dbias", lambda E: E.dma_start(out=ob_s[:], in_=out_b), writes=[Rob])
    dmax("dbias", lambda E: E.dma_start(out=lb_s[:], in_=lin2_b), writes=[Rlb])
    dmax("dbias", lambda E: E.dma_start(out=rb_s[:], in_=router_b), writes=[Rrb])
    dmaw("dmisc", lambda E: E.dma_start(out=rw_s[:], in_=router_w), writes=[Rrw])
    dmaw("dmisc", lambda E: E.dma_start(out=gt_tm[64:128, :], in_=gmem), writes=[Rgt_tm])
    dmaw("dmisc", lambda E: E.dma_start(out=cbrow[:], in_=convw[4096:4097, :]), writes=[Rcbrow])
    conv_src = convw[0:4096, :].rearrange("(j p) c -> p j c", p=128)

    def conv_group_load(g):
        slot = g % 2
        dst = w2a[:, slot * 4096:(slot + 1) * 4096].rearrange("p (j c) -> p j c", c=1024)
        dmaw("dw2a" if slot == 0 else "dw2b",
             lambda E, g=g, d=dst: E.dma_start(out=d, in_=conv_src[:, 4 * g:4 * g + 4, :]),
             writes=[Rconv[slot]])

    for r in (Rgb, Rob, Rlb, Rrb):
        r.writers = {"dbias": P.cnt["dbias"]}
    for r in (Rrw, Rgt_tm, Rcbrow):
        r.writers = {"dmisc": P.cnt["dmisc"]}

    conv_group_load(0)
    conv_group_load(1)
    gw_src = gate_w.rearrange("(k p) c -> p k c", p=128)
    w1_src = w1f.rearrange("(k p) c -> p k c", p=128)
    l2_src = lin2_w.rearrange("(k p) c -> p k c", p=128)
    w2_src = w2f.rearrange("(k p) (e c) -> p e k c", p=128, c=512)
    gwb_src = gw_bf.rearrange("(k p) c -> p k c", p=128)
    owb_src = ow_bf.rearrange("(k p) c -> p k c", p=128)
    w1b_src = w1_bf.rearrange("(k p) c -> p k c", p=128)
    l2b_src = l2_bf.rearrange("(k p) c -> p k c", p=128)
    w2b_src = w2_bf.rearrange("(k p) (e c) -> p e k c", p=128, c=512)

    def wb_store(sbuf_view, dst_view, slab_reg, scr_reg, sem="dsb"):
        P.emit("sync", lambda E, s=sbuf_view, d=dst_view: E.dma_start(out=d, in_=s),
               reads=[slab_reg], writes=[scr_reg], inc=sem, k=16)

    dmaw("dgw1", lambda E: E.dma_start(out=gw1a.rearrange("p (k c) -> p k c", c=1024),
                               in_=gw_src),
         writes=[Rgw1])
    wb_store(gw1a.rearrange("p (k c) -> p k c", c=1024), gwb_src, Rgw1, Rsgw)
    ow_src = out_w.rearrange("(k p) c -> p k c", p=128)
    dmaw("doutw", lambda E: E.dma_start(out=outwa.rearrange("p (k c) -> p k c", c=1024),
                                in_=ow_src),
         writes=[Routw])
    wb_store(outwa.rearrange("p (k c) -> p k c", c=1024), owb_src, Routw, Rsow)
    for r in (Rsgw, Rsow):
        r.writers = {"dsb": P.cnt["dsb"]}
    Rgw1.readers["dsb"] = P.cnt["dsb"]
    Routw.readers["dsb"] = P.cnt["dsb"]


    def ln_stage(sc_base, out_kind, t, bank_ids, scratch_reg):
        bsy, bsq, bb0, bb1 = bank_ids
        srct, srcr = (xy, Rxy) if out_kind == "ln1" else (ynbf, Rynbf)
        for fc in range(NCH):
            xsl = srct[:, fc * 1024 + sc_base: fc * 1024 + sc_base + 512]
            pl = ah[:, PL0 + (fc % 2) * 512: PL0 + (fc % 2) * 512 + 512]
            srg = scratch_reg(fc)
            sx(lambda E, i=xsl, o=pl: E.activation(o, i, AF.Square),
               reads=[srcr], writes=[srg], inc="S")
            tx(lambda E, r=xsl, fc=fc: E.matmul(pb[bsy][0:1, :], ones_b[:, 0:1], r,
                                                start=(fc == 0), stop=(fc == 7)),
               reads=[srcr, Rconst], writes=[Rb[bsy]], inc="T")
            tx(lambda E, r=pl, fc=fc: E.matmul(pb[bsq][0:1, :], ones_b[:, 0:1], r,
                                               start=(fc == 0), stop=(fc == 7)),
               reads=[srg, Rconst], writes=[Rb[bsq]], inc="T")
        mean = statv[0:1, 0:512]
        inv = statv[0:1, 512:1024]
        vx(lambda E: E.tensor_scalar(mean, pb[bsy][0:1, :], 1.0 / 1024, None, ALU.mult),
           reads=[Rb[bsy]], writes=[Rstat], inc="V")
        vx(lambda E: E.tensor_mul(inv, mean, mean), reads=[Rstat], writes=[Rstat],
           inc="V")
        vx(lambda E: E.scalar_tensor_tensor(inv, pb[bsq][0:1, :], 1.0 / 1024, inv,
                                            ALU.mult, ALU.subtract),
           reads=[Rb[bsq], Rstat], writes=[Rstat], inc="V")
        sx(lambda E: E.activation(inv, inv, AF.Ln, bias=epsc[0:1, :]),
           reads=[Rstat, Rconst], writes=[Rstat], inc="S")
        sx(lambda E: E.activation(inv, inv, AF.Exp, scale=-0.5),
           reads=[Rstat], writes=[Rstat], inc="S")
        invb = statv.bitcast(BF16)[0:1, 2048:2560]
        minvb = statv.bitcast(BF16)[0:1, 2560:3072]
        vx(lambda E: E.tensor_copy(invb, inv), reads=[Rstat], writes=[Rstat], inc="V")
        vx(lambda E: E.tensor_mul(mean, mean, inv), reads=[Rstat], writes=[Rstat],
           inc="V")
        vx(lambda E: E.tensor_copy(minvb, mean), reads=[Rstat], writes=[Rstat],
           inc="V")
        tx(lambda E: E.matmul(pb[bb0][:], ones_b[0:1, :], invb, start=True, stop=True),
           reads=[Rstat, Rconst], writes=[Rb[bb0]], inc="T")
        tx(lambda E: E.matmul(pb[bb1][:], ones_b[0:1, :], minvb, start=True, stop=True),
           reads=[Rstat, Rconst], writes=[Rb[bb1]], inc="T")
        for fc in range(NCH):
            xsl = srct[:, fc * 1024 + sc_base: fc * 1024 + sc_base + 512]
            q1 = scrf[:, 0:512]
            vx(lambda E, i=xsl, o=q1: E.tensor_mul(o, i, pb[bb0][:]),
               reads=[srcr, Rb[bb0]], writes=[Rscrf], inc="V")
            if out_kind == "ln1":
                osl = ynbf[:, fc * 1024 + sc_base: fc * 1024 + sc_base + 512]
                vx(lambda E, i=q1, o=osl: E.tensor_sub(o, i, pb[bb1][:]),
                   reads=[Rscrf, Rb[bb1]], writes=[Rynbf], inc="V")
            else:
                slot = 0
                osl = outst[:, 0:512]
                vx(lambda E, i=q1, o=osl: E.tensor_sub(o, i, pb[bb1][:]),
                   reads=[Rscrf, Rb[bb1]], writes=[Rout[slot]], inc="V")
                dst = out_fm[fc * 128:(fc + 1) * 128,
                             t * 1024 + sc_base: t * 1024 + sc_base + 512]
                P.emit("sync", lambda E, s=osl, d=dst: E.dma_start(out=d, in_=s),
                       reads=[Rout[slot]], inc="dout%d" % slot, k=16)

    for t in range(NT):
        tb0 = t * 1024
        if t > 0:
            alias_edge(Rpl + Rpg + Rrz + [Rloc, Rglo, Rmix], [Rah])
        xfm_g = x_fm.rearrange("(f p) s -> p f s", p=128)
        pfm_g = pos_fm.rearrange("(f p) s -> p f s", p=128)
        xtm_g = x_tm.rearrange("(g p) d -> p g d", p=128)
        ptm_g = pos_tm.rearrange("(g p) d -> p g d", p=128)
        def emit_A(tt_, sc_):
            base = tt_ * 1024 + sc_ * 512
            if tt_ >= NT:
                return
            dmaw("dpf0", lambda E, s=xfm_g[:, :, base: base + 512],
                 d=xpef.rearrange("p (f s) -> p f s", s=512): E.dma_start(
                     out=d, in_=s),
                 writes=[Rxpef])
            dmaw("dpf1", lambda E, s=pfm_g[:, :, base: base + 512],
                 d=xpef.rearrange("p (f s) -> p f s", s=512): E.dma_start(
                     out=d, in_=s, accum_op=ALU.add),
                 writes=[Rxpef])
            g0 = base // 128
            dmaw("dsl0", lambda E, s=xtm_g[:, g0:g0 + 4, :],
                 d=xpet.rearrange("p (g d) -> p g d", d=1024): E.dma_start(
                     out=d, in_=s),
                 writes=[Rxpet])
            dmaw("dsl1", lambda E, s=ptm_g[:, g0:g0 + 4, :],
                 d=xpet.rearrange("p (g d) -> p g d", d=1024): E.dma_start(
                     out=d, in_=s, accum_op=ALU.add),
                 writes=[Rxpet])

        for sc in range(2):
            scb = sc * 512
            if t == 0 and sc == 0:
                emit_A(0, 0)
            if sc == 0:
                for fcx in range(NCH):
                    srcx = x_fm[fcx * 128:(fcx + 1) * 128, tb0:tb0 + 1024]
                    dstx = xy[:, fcx * 1024: fcx * 1024 + 1024]
                    dmaw("dxy", lambda E, s=srcx, d=dstx: E.dma_start(out=d, in_=s),
                         writes=[Rxy])

            scnt = 0
            pcnt = 0
            for q in range(4):
                qs = q % 2
                for hp in range(2):
                    hA = q * 4 + hp * 2
                    fc = hA // 2
                    pair0 = pcnt
                    def emit_av(hh2, pbase, kc):
                        h = q * 4 + hh2
                        av_bank = 3 + hh2
                        slot = (pair0 + kc * 2 + (hh2 - hp * 2)) % 4
                        pls = ah[:, PL0 + slot * 512: PL0 + slot * 512 + 512]
                        lhs = xpet[:, kc * 1024 + 64 * h:
                                   kc * 1024 + 64 * h + 64]
                        out = pb[av_bank][pbase:pbase + 64, :]
                        tx(lambda E, l=lhs, r=pls, o=out, kc=kc: E.matmul(
                            o, l, r, start=(kc == 0), stop=(kc == 3)),
                           reads=[Rpl[slot], Rxpet], writes=[Rb[av_bank]],
                           inc=("T" if kc == 3 else None))

                    for kc in range(4):
                        for hh2, pbase in ((hp * 2, 0), (hp * 2 + 1, 64)):
                            h = q * 4 + hh2
                            b = scnt % 3
                            scnt += 1
                            lhs = xpef[pbase:pbase + 64,
                                       fc * 512 + kc * 128:
                                       fc * 512 + kc * 128 + 128]
                            rhs = xpef[pbase:pbase + 64,
                                       fc * 512: fc * 512 + 512]
                            tx(lambda E, l=lhs, r=rhs, b=b: E.matmul(
                                pb[b][:], l, r, start=True, stop=True),
                               reads=[Rxpef], writes=[Rb[b]], inc="T")
                            slot = pcnt % 4
                            pcnt += 1
                            pls = ah[:, PL0 + slot * 512: PL0 + slot * 512 + 512]
                            zc = ztm[:, qs * 16 + hh2 * 4 + kc:
                                     qs * 16 + hh2 * 4 + kc + 1]
                            sx(lambda E, b=b, o=pls, z=zc: E.activation(
                                o, pb[b][:], AF.Exp, bias=neg8[:], scale=SCALE,
                                accum_out=z),
                               reads=[Rb[b], Rconst], writes=[Rpl[slot], Rzt[qs]],
                               inc="S")
                        if kc >= 1:
                            emit_av(hp * 2, 0, kc - 1)
                            emit_av(hp * 2 + 1, 64, kc - 1)
                    emit_av(hp * 2, 0, 3)
                    emit_av(hp * 2 + 1, 64, 3)
                vx(lambda E, qs=qs: E.reciprocal(rztm[:, qs * 16: qs * 16 + 16],
                                                 ztm[:, qs * 16: qs * 16 + 16]),
                   reads=[Rzt[qs]], writes=[Rrzt[qs]], inc="V")
                for hh in range(4):
                    h = q * 4 + hh
                    fc, pbase = h // 2, 64 * (h % 2)
                    hbase = (h % 2) * 64
                    for kc in range(4):
                        rc = rztm[:, qs * 16 + hh * 4 + kc:
                                  qs * 16 + hh * 4 + kc + 1].broadcast_to((128, 64))
                        out = pb[7][hbase:hbase + 64, kc * 128:kc * 128 + 128]
                        tx(lambda E, i=rc, o=out, kc=kc: E.matmul(
                            o, i, id_b[:], start=(kc == 0), stop=(kc == 3)),
                           reads=[Rrzt[qs], Rconst], writes=[Rb[7]],
                           inc=("T" if kc == 3 else None))
                    if hh % 2 == 1:
                        vx(lambda E: E.tensor_copy(csb[:], pb[7][:]),
                           reads=[Rb[7]], writes=[Rcsb], inc="V")
                        avA, avB = 3 + hh - 1, 3 + hh
                        lA = ah[0:64, LOC + fc * 512: LOC + fc * 512 + 512]
                        lB = ah[64:128, LOC + fc * 512: LOC + fc * 512 + 512]
                        vx(lambda E, o=lA, ab=avA: E.tensor_mul(
                            o, pb[ab][0:64, :], csb[0:64, :]),
                           reads=[Rb[avA], Rcsb], writes=[Rloc], inc="V")
                        vx(lambda E, o=lB, ab=avB: E.tensor_mul(
                            o, pb[ab][64:128, :], csb[64:128, :]),
                           reads=[Rb[avB], Rcsb], writes=[Rloc], inc="V")

            if t == 0 and sc == 0:
                for j in range(32):
                    kc, ic = j // 8, j % 8
                    g, jj = j // 4, j % 4
                    slot = g % 2
                    if jj == 0 and g >= 2:
                        conv_group_load(g)
                    lhs = xpef[:, ic * 512: ic * 512 + 256].rearrange(
                        "p (t f) -> p t f", f=4)[:, :, kc]
                    for oh in range(2):
                        bank = 3 + oh
                        rhs = w2a[:, slot * 4096 + jj * 1024 + oh * 512:
                                  slot * 4096 + jj * 1024 + oh * 512 + 512]
                        tx(lambda E, l=lhs, r=rhs, b=bank, j=j: E.matmul(
                            pb[b][0:64, :], l, r, start=(j == 0), stop=False),
                           reads=[Rxpef, Rconv[slot]], writes=[Rb[bank]],
                           inc=("T" if (j % 4) == 3 and oh == 1 else None))
                for oh in range(2):
                    bank = 3 + oh
                    rhsb = cbrow[0:1, oh * 512: oh * 512 + 512]
                    tx(lambda E, r=rhsb, b=bank: E.matmul(
                        pb[b][0:64, :], ones_b[0:1, 0:64], r, start=False, stop=True),
                       reads=[Rcbrow, Rconst], writes=[Rb[bank]], inc="T")
                    vx(lambda E, b=bank, oh=oh: E.tensor_copy(
                        gt_tm[0:64, oh * 512: oh * 512 + 512], pb[b][0:64, :]),
                       reads=[Rb[bank]], writes=[Rgt_tm], inc="V")
                for fc in range(NCH):
                    bank = 5
                    tin = gt_tm[:, fc * 128: fc * 128 + 128]
                    tx(lambda E, i=tin, b=bank: E.matmul(
                        pbb[b][:, 0:128], i, id_b[:], start=True, stop=True,
                        is_transpose=True),
                       reads=[Rgt_tm, Rconst], writes=[Rb[bank]], inc="T")
                    vx(lambda E, fc=fc, b=bank: E.tensor_copy(
                        gt_fm[:, fc * 128: fc * 128 + 128], pbb[b][:, 0:128]),
                       reads=[Rb[bank]], writes=[Rgt_fm], inc="V")

            scnt = 0
            for q in range(4):
                qs = q % 2
                for hh in range(4):
                    h = q * 4 + hh
                    fc, pbase = h // 2, 64 * (h % 2)
                    b = scnt % 2
                    scnt += 1
                    lhs = gt_fm[pbase:pbase + 64, fc * 128: fc * 128 + 128]
                    rhs = xpef[pbase:pbase + 64, fc * 512: fc * 512 + 512]
                    tx(lambda E, l=lhs, r=rhs, b=b: E.matmul(
                        pb[b][:], l, r, start=True, stop=True),
                       reads=[Rgt_fm, Rxpef], writes=[Rb[b]], inc="T")
                    slot = h % 2
                    pgs = ah[:, PG0 + slot * 512: PG0 + slot * 512 + 512]
                    sx(lambda E, b=b, o=pgs: E.activation(o, pb[b][:], AF.Exp,
                                                          bias=neg8[:], scale=SCALE),
                       reads=[Rb[b], Rconst], writes=[Rpg[slot]], inc="S")
                    for tt in range(4):
                        zout = pb[2][:, qs * 16 + hh * 4 + tt: qs * 16 + hh * 4 + tt + 1]
                        tx(lambda E, l=pgs, o=zout, tt=tt, hh=hh, qs=qs: E.matmul(
                            o, l[:, tt * 128:tt * 128 + 128], ones_b[:, 0:1],
                            start=(hh == 0 and tt == 0), stop=(hh == 3 and tt == 3)),
                           reads=[Rpg[slot], Rconst], writes=[Rb[2]],
                           inc=("T" if (hh == 3 and tt == 3) else None))
                    av_bank = 3 + hh
                    lhs2 = gt_tm[:, h * 64: h * 64 + 64]
                    out = pb[av_bank][pbase:pbase + 64, :]
                    tx(lambda E, l=lhs2, r=pgs, o=out: E.matmul(
                        o, l, r, start=True, stop=True),
                       reads=[Rpg[slot], Rgt_tm], writes=[Rb[av_bank]], inc="T")
                vx(lambda E, qs=qs: E.reciprocal(rztm[:, qs * 16: qs * 16 + 16],
                                                 pb[2][:, qs * 16: qs * 16 + 16]),
                   reads=[Rb[2]], writes=[Rrzt[qs]], inc="V")
                for hh in range(4):
                    h = q * 4 + hh
                    fc, pbase = h // 2, 64 * (h % 2)
                    hbase = (h % 2) * 64
                    for kc in range(4):
                        rc = rztm[:, qs * 16 + hh * 4 + kc:
                                  qs * 16 + hh * 4 + kc + 1].broadcast_to((128, 64))
                        out = pb[7][hbase:hbase + 64, kc * 128:kc * 128 + 128]
                        tx(lambda E, i=rc, o=out, kc=kc: E.matmul(
                            o, i, id_b[:], start=(kc == 0), stop=(kc == 3)),
                           reads=[Rrzt[qs], Rconst], writes=[Rb[7]],
                           inc=("T" if kc == 3 else None))
                    if hh % 2 == 1:
                        vx(lambda E: E.tensor_copy(csb[:], pb[7][:]),
                           reads=[Rb[7]], writes=[Rcsb], inc="V")
                        avA, avB = 3 + hh - 1, 3 + hh
                        gA = ah[0:64, GLO + fc * 512: GLO + fc * 512 + 512]
                        gB = ah[64:128, GLO + fc * 512: GLO + fc * 512 + 512]
                        vx(lambda E, o=gA, ab=avA: E.tensor_mul(
                            o, pb[ab][0:64, :], csb[0:64, :]),
                           reads=[Rb[avA], Rcsb], writes=[Rglo], inc="V")
                        vx(lambda E, o=gB, ab=avB: E.tensor_mul(
                            o, pb[ab][64:128, :], csb[64:128, :]),
                           reads=[Rb[avB], Rcsb], writes=[Rglo], inc="V")

            emit_A(t + (sc == 1), (sc + 1) % 2)

            for m in range(NCH):
                bank = m % 3
                for k in range(16):
                    lhs = gw1a[:, k * 1024 + m * 128: k * 1024 + m * 128 + 128]
                    base = LOC if k < 8 else GLO
                    rr = Rloc if k < 8 else Rglo
                    rhs = ah[:, base + (k % 8) * 512: base + (k % 8) * 512 + 512]
                    tx(lambda E, l=lhs, r=rhs, b=bank, k=k: E.matmul(
                        pb[b][:], l, r, start=(k == 0), stop=(k == 15)),
                       reads=[Rgw1, rr], writes=[Rb[bank]],
                       inc=("T" if k == 15 else None))
                msl = ah[:, MIX + m * 512: MIX + m * 512 + 512]
                sx(lambda E, b=bank, o=msl, m=m: E.activation(
                    o, pb[b][:], AF.Sigmoid, bias=gb_s[:, m:m + 1]),
                   reads=[Rb[bank], Rgb], writes=[Rmix], inc="S")
            for fc in range(NCH):
                lsl = ah[:, LOC + fc * 512: LOC + fc * 512 + 512]
                gsl = ah[:, GLO + fc * 512: GLO + fc * 512 + 512]
                msl = ah[:, MIX + fc * 512: MIX + fc * 512 + 512]
                t1 = ah[:, PL0 + (fc % 4) * 512: PL0 + (fc % 4) * 512 + 512]
                gp(lambda E, a=lsl, b=gsl, o=t1: E.tensor_sub(o, a, b),
                   reads=[Rloc, Rglo], writes=[Rpl[fc % 4]], inc="G")
                gp(lambda E, a=t1, b=msl, o=t1: E.tensor_mul(o, a, b),
                   reads=[Rpl[fc % 4], Rmix], writes=[Rpl[fc % 4]], inc="G")
                gp(lambda E, a=t1, b=gsl, o=msl: E.tensor_add(o, a, b),
                   reads=[Rpl[fc % 4], Rglo], writes=[Rmix], inc="G")
            for m in range(NCH):
                bank = 3 + (m % 3)
                for k in range(NCH):
                    lhs = outwa[:, k * 1024 + m * 128: k * 1024 + m * 128 + 128]
                    rhs = ah[:, MIX + k * 512: MIX + k * 512 + 512]
                    tx(lambda E, l=lhs, r=rhs, b=bank, k=k: E.matmul(
                        pb[b][:], l, r, start=(k == 0), stop=(k == 7)),
                       reads=[Routw, Rmix], writes=[Rb[bank]],
                       inc=("T" if k == 7 else None))
                xsl = xy[:, m * 1024 + scb: m * 1024 + scb + 512]
                vx(lambda E, b=bank, o=xsl, m=m: E.scalar_tensor_tensor(
                    o, pb[b][:], ob_s[:, m:m + 1], o, ALU.add, ALU.add),
                   reads=[Rb[bank], Rob, Rxy], writes=[Rxy], inc="V")
            ln_stage(scb, "ln1", t, (6, 7, 0, 1), lambda fc: Rpl[fc % 2])

        alias_edge([Rah], Rpl + Rpg + Rrz + [Rloc, Rglo, Rmix])
        if t == 0:
            dmaw("dgw1", lambda E: E.dma_start(
                out=gw1a.rearrange("p (k c) -> p k c", c=2048), in_=w1_src),
                 writes=[Rgw1])
            wb_store(gw1a.rearrange("p (k c) -> p k c", c=2048), w1b_src, Rgw1, Rsw1)
        else:
            P.emit("gpsimd", lambda E: E.dma_start(
                out=gw1a.rearrange("p (k c) -> p k c", c=2048), in_=w1b_src),
                reads=[Rsw1], writes=[Rgw1], inc="dgw1", k=16)
        if t == 0:
            alias_edge(Rw2, Rconv)

        for sc in range(2):
            scb = sc * 512
            bank_lg = 2
            for fc in range(NCH):
                lhs = rw_s[:, fc * 8: fc * 8 + 8]
                rhs = ynbf[:, fc * 1024 + scb: fc * 1024 + scb + 512]
                tx(lambda E, l=lhs, r=rhs, fc=fc: E.matmul(
                    pb[bank_lg][0:8, :], l, r, start=(fc == 0), stop=(fc == 7)),
                   reads=[Rrw, Rynbf], writes=[Rb[bank_lg]],
                   inc=("T" if fc == 7 else None))
            sx(lambda E: E.activation(ebf[0:8, :], pb[bank_lg][0:8, :], AF.Exp,
                                      bias=rb_s[:]),
               reads=[Rb[bank_lg], Rrb], writes=[Rebf], inc="S")
            bank_et = 3
            for u in range(4):
                tin = ebf[0:8, u * 128: u * 128 + 128]
                out = pb[bank_et][:, u * 8: u * 8 + 8]
                tx(lambda E, i=tin, o=out, u=u: E.matmul(
                    o, i, id_f[0:8, 0:8], start=(u == 0), stop=(u == 3),
                    is_transpose=True),
                   reads=[Rebf, Rconst], writes=[Rb[bank_et]],
                   inc=("T" if u == 3 else None))
            et = pb[bank_et][:, 0:32].rearrange("p (a b) -> p a b", b=8)
            zt = scr4[:, 8:12]
            rzt = scr4[:, 12:16]
            m1 = scr4[:, 0:4]
            m2 = scr4[:, 4:8]
            eq = scrf[:, 512:544]
            dd = scrf[:, 544:576]
            vx(lambda E: E.reduce_sum(zt, et, axis=AX.X), reads=[Rb[bank_et]],
               writes=[Rscr4], inc="V")
            vx(lambda E: E.reciprocal(rzt, zt), reads=[Rscr4], writes=[Rscr4], inc="V")
            vx(lambda E: E.reduce_max(m1, et, axis=AX.X), reads=[Rb[bank_et]],
               writes=[Rscr4], inc="V")
            m1b = m1.rearrange("p (a b) -> p a b", b=1).broadcast_to((128, 4, 8))
            vx(lambda E: E.tensor_tensor(eq.rearrange("p (a b) -> p a b", b=8), et, m1b,
                                         ALU.is_equal),
               reads=[Rb[bank_et], Rscr4], writes=[Rscrf], inc="V")
            vx(lambda E: E.scalar_tensor_tensor(dd, eq, -1e9, et.rearrange("p a b -> p (a b)"),
                                                ALU.mult, ALU.add),
               reads=[Rscrf, Rb[bank_et]], writes=[Rscrf], inc="V")
            vx(lambda E: E.reduce_max(m2, dd.rearrange("p (a b) -> p a b", b=8), axis=AX.X),
               reads=[Rscrf], writes=[Rscr4], inc="V")
            m2b = m2.rearrange("p (a b) -> p a b", b=1).broadcast_to((128, 4, 8))
            vx(lambda E: E.tensor_tensor(eq.rearrange("p (a b) -> p a b", b=8), et, m2b,
                                         ALU.is_ge),
               reads=[Rb[bank_et], Rscr4], writes=[Rscrf], inc="V")
            vx(lambda E: E.tensor_tensor(dd.rearrange("p (a b) -> p a b", b=8),
                                         eq.rearrange("p (a b) -> p a b", b=8), et,
                                         ALU.mult),
               reads=[Rscrf, Rb[bank_et]], writes=[Rscrf], inc="V")
            rztb = rzt.rearrange("p (a b) -> p a b", b=1).broadcast_to((128, 4, 8))
            vx(lambda E: E.tensor_tensor(ctm.rearrange("p (a b) -> p a b", b=8),
                                         dd.rearrange("p (a b) -> p a b", b=8), rztb,
                                         ALU.mult),
               reads=[Rscrf, Rscr4], writes=[Rctm], inc="V")
            bank_cf = 4
            for u in range(4):
                tin = ctm[:, u * 8: u * 8 + 8]
                out = pb[bank_cf][0:8, u * 128: u * 128 + 128]
                tx(lambda E, i=tin, o=out, u=u: E.matmul(
                    o, i, id_f[:], start=(u == 0), stop=(u == 3),
                    is_transpose=True),
                   reads=[Rctm, Rconst], writes=[Rb[bank_cf]],
                   inc=("T" if u == 3 else None))
            vx(lambda E: E.tensor_copy(cfm[0:8, :], pb[bank_cf][0:8, 0:512]),
               reads=[Rb[bank_cf]], writes=[Rcfm], inc="V")

            for er in range(16):
                e = er // 2
                mslot = 0
                msb = mwsb[:, 0:512]
                if er % 2 == 0:
                    mwb_bank = 5 + mslot
                    lhs = basis[0:8, e * 128: e * 128 + 128]
                    tx(lambda E, l=lhs, b=mwb_bank: E.matmul(pb[b][:], l, cfm[0:8, :],
                                                             start=True, stop=True),
                       reads=[Rcfm, Rconst], writes=[Rb[mwb_bank]], inc="T")
                    sx(lambda E, b=mwb_bank, o=msb: E.copy(o, pb[b][:]),
                       reads=[Rb[mwb_bank]], writes=[Rmwsb[mslot]], inc="S")
                bank = er % 3
                for k in range(NCH):
                    lhs = gw1a[:, k * 2048 + er * 128: k * 2048 + er * 128 + 128]
                    rhs = ynbf[:, k * 1024 + scb: k * 1024 + scb + 512]
                    tx(lambda E, l=lhs, r=rhs, b=bank, k=k: E.matmul(
                        pb[b][:], l, r, start=(k == 0), stop=(k == 7)),
                       reads=[Rgw1, Rynbf], writes=[Rb[bank]],
                       inc=("T" if k == 7 else None))
                tsl = tbuf[:, er * 512: er * 512 + 512]
                vx(lambda E, b=bank, o=tsl, m=msb: E.tensor_mul(o, pb[b][:], m),
                   reads=[Rb[bank], Rmwsb[mslot]], writes=[Rtb], inc="V")

            for e8 in range(8):
                reg = e8 % 2
                dst = w2a[:, reg * 8192: reg * 8192 + 8192].rearrange(
                    "p (k c) -> p k c", c=512)
                if t == 0 and sc == 0:
                    dmaw("dw2a" if reg == 0 else "dw2b",
                         lambda E, d=dst, e8=e8: E.dma_start(out=d, in_=w2_src[:, e8]),
                         writes=[Rw2[reg]])
                    wb_store(dst, w2b_src[:, e8], Rw2[reg], Rsw2,
                             "dsw2" if reg == 0 else "dsw2b")
                else:
                    P.emit("gpsimd",
                           lambda E, d=dst, e8=e8: E.dma_start(out=d, in_=w2b_src[:, e8]),
                           reads=[Rsw2], writes=[Rw2[reg]],
                           inc=("dw2a" if reg == 0 else "dw2b"), k=16)
                for f4 in range(4):
                    f = e8 * 4 + f4
                    bank = f % 3
                    for k in range(16):
                        lhs = w2a[:, reg * 8192 + k * 512 + f4 * 128:
                                  reg * 8192 + k * 512 + f4 * 128 + 128]
                        rhs = tbuf[:, k * 512: k * 512 + 512]
                        tx(lambda E, l=lhs, r=rhs, b=bank, k=k: E.matmul(
                            pb[b][:], l, r, start=(k == 0), stop=(k == 15)),
                           reads=[Rw2[reg], Rtb], writes=[Rb[bank]],
                           inc=("T" if k == 15 else None))
                    hsl = ah[:, f * 512: f * 512 + 512]
                    sx(lambda E, b=bank, o=hsl: E.activation(o, pb[b][:], AF.Gelu),
                       reads=[Rb[bank]], writes=[Rah], inc="S")

            if t == 0 and sc == 0:
                Rsw2.writers = {"dsw2": P.cnt["dsw2"], "dsw2b": P.cnt["dsw2b"]}
            if t == 0 and sc == 0:
                alias_edge([Rl2[0]], [Rcbrow])
            for kf in range(32):
                slot = (kf // 2) % 2
                if kf % 2 == 0:
                    dst = l2a[:, slot * 2048: slot * 2048 + 2048].rearrange(
                        "p (k c) -> p k c", c=1024)
                    if t == 0 and sc == 0:
                        dmaw("dl2a" if slot == 0 else "dl2b",
                             lambda E, d=dst, kf=kf: E.dma_start(
                                 out=d, in_=l2_src[:, kf:kf + 2, :]),
                             writes=[Rl2[slot]])
                        wb_store(dst, l2b_src[:, kf:kf + 2, :], Rl2[slot], Rsl2,
                                 "dsl2s" if slot == 0 else "dsl2t")
                    else:
                        P.emit("gpsimd",
                               lambda E, d=dst, kf=kf: E.dma_start(
                                   out=d, in_=l2b_src[:, kf:kf + 2, :]),
                               reads=[Rsl2], writes=[Rl2[slot]],
                               inc=("dl2a" if slot == 0 else "dl2b"), k=16)
                off = (kf % 2) * 1024
                for m in range(NCH):
                    lhs = l2a[:, slot * 2048 + off + m * 128:
                              slot * 2048 + off + m * 128 + 128]
                    rhs = ah[:, kf * 512: kf * 512 + 512]
                    tx(lambda E, l=lhs, r=rhs, b=m, kf=kf: E.matmul(
                        pb[b][:], l, r, start=(kf == 0), stop=(kf == 31)),
                       reads=[Rl2[slot], Rah], writes=[Rb[m]],
                       inc=("T" if (kf % 2 == 1 and m == NCH - 1) else None))
            if t == 0 and sc == 0:
                Rsl2.writers = {"dsl2s": P.cnt["dsl2s"], "dsl2t": P.cnt["dsl2t"]}
            for m in range(NCH):
                ysl = ynbf[:, m * 1024 + scb: m * 1024 + scb + 512]
                vx(lambda E, b=m, y=ysl, m=m: E.scalar_tensor_tensor(
                    y, pb[b][:], lb_s[:, m:m + 1], y, ALU.add, ALU.add),
                   reads=[Rb[m], Rlb, Rynbf], writes=[Rynbf], inc="V")
            ln_stage(scb, "ln2", t, (0, 1, 2, 3), lambda fc: Rah)

        if t + 1 < NT:
            P.emit("gpsimd", lambda E: E.dma_start(
                out=gw1a.rearrange("p (k c) -> p k c", c=1024), in_=gwb_src),
                reads=[Rsgw], writes=[Rgw1], inc="dgw1", k=16)
            P.emit("gpsimd", lambda E: E.dma_start(
                out=outwa.rearrange("p (k c) -> p k c", c=1024), in_=owb_src),
                reads=[Rsow], writes=[Routw], inc="doutw", k=16)

    P.raw_wait("sync", {"dout0": P.cnt["dout0"], "dout1": P.cnt["dout1"]})

    semh = {}
    for name in ("T", "V", "S", "G") + DMA_SEMS:
        semh[name] = stack.enter_context(nc.semaphore(name))

    ctx_lp = nc.allow_low_precision(
        reason="bf16 operands for matmuls are intended; fp32 PSUM accumulation")
    ctx_lp.__enter__()
    with nc.Block() as block:
        def run_queue(qname):
            def runner(E):
                for item in P.q[qname]:
                    if item[0] == "w":
                        E.wait_ge(semh[item[1]], item[2])
                    else:
                        _, fn, inc, k = item
                        ins = fn(E)
                        if inc is not None:
                            ins.then_inc(semh[inc], k)
            return runner

        block.tensor(run_queue("tensor"))
        block.vector(run_queue("vector"))
        block.scalar(run_queue("scalar"))
        block.sync(run_queue("sync"))
        block.gpsimd(run_queue("gpsimd"))

    ctx_lp.__exit__(None, None, None)
    stack.close()
    return nc


_NC_CACHE = None


def kernel(L0_attn_x, L0_attn_pos_emb, L0_attn_global_memory, L0_attn_conv_w,
           L0_attn_conv_b, L0_attn_gate_w, L0_attn_gate_b, L0_attn_out_w,
           L0_attn_out_b, L0_ffn_router_w, L0_ffn_router_b, L0_ffn_w1,
           L0_ffn_w2, L0_ffn_lin2_w, L0_ffn_lin2_b):
    global _NC_CACHE
    if _NC_CACHE is None:
        _NC_CACHE = build_nc()
    nc = _NC_CACHE

    f32 = np.float32
    ac = np.ascontiguousarray
    pos = ac(L0_attn_pos_emb[:S], dtype=f32)
    pos_f = ac(pos.T)
    convw = np.concatenate(
        [L0_attn_conv_w.transpose(2, 1, 0).reshape(4096, D),
         L0_attn_conv_b.reshape(1, D)], axis=0)
    convw = ac(convw, dtype=f32)
    gate_w = ac(L0_attn_gate_w, dtype=f32)
    out_w = ac(L0_attn_out_w, dtype=f32)
    w1fh = ac(L0_ffn_w1.transpose(1, 0, 2).reshape(D, 2048), dtype=f32)
    w2fh = ac(L0_ffn_w2.reshape(2048, FF), dtype=f32)
    lin2 = ac(L0_ffn_lin2_w, dtype=f32)
    gb = ac(L0_attn_gate_b.reshape(8, 128).T, dtype=f32)
    ob = ac(L0_attn_out_b.reshape(8, 128).T, dtype=f32)
    lb = ac(L0_ffn_lin2_b.reshape(8, 128).T, dtype=f32)
    rw = ac(L0_ffn_router_w.reshape(8, 128, 8).transpose(1, 0, 2).reshape(128, 64),
            dtype=f32)
    rb = ac(L0_ffn_router_b.reshape(8, 1), dtype=f32)
    gm = ac(L0_attn_global_memory[0], dtype=f32)

    in_maps = []
    for b in range(8):
        xb = ac(L0_attn_x[b], dtype=f32)
        in_maps.append({
            "x_tm": xb, "x_fm": ac(xb.T),
            "pos_tm": pos, "pos_fm": pos_f,
            "gmem": gm, "convw": convw,
            "gate_w": gate_w, "out_w": out_w,
            "w1f": w1fh, "w2f": w2fh, "lin2_w": lin2,
            "gate_b": gb, "out_b": ob, "lin2_b": lb,
            "router_w": rw, "router_b": rb,
        })
    res = run_bass_kernel_spmd(nc, in_maps, core_ids=list(range(8)))
    out = np.empty((8, S, D), np.float32)
    for b in range(8):
        out[b] = res.results[b]["out_fm"].T
    return out


# version 15
# speedup vs baseline: 1.1103x; 1.1103x over previous
import numpy as np
import concourse.bass as bass
import concourse.mybir as mybir
from concourse.bass_utils import run_bass_kernel_spmd

F32 = mybir.dt.float32
BF16 = mybir.dt.bfloat16
AF = mybir.ActivationFunctionType
ALU = mybir.AluOpType
AX = mybir.AxisListType

S, D, H, HD, WIN = 4096, 1024, 16, 64, 512
TILE, NT, SC = 1024, 4, 512
NCH = 8
E8, RR, FF = 8, 256, 4096
SCALE = 0.125
CEXP = -8.0

ENGINES = ("tensor", "vector", "scalar", "sync", "gpsimd")


class Reg:
    __slots__ = ("writers", "readers")

    def __init__(self):
        self.writers = {}
        self.readers = {}

    def wrote(self, gate):
        s, v = gate
        self.writers = {s: v}
        self.readers = {}

    def read(self, gate):
        s, v = gate
        if self.readers.get(s, 0) < v:
            self.readers[s] = v

def alias_edge(dsts, srcs):
    merged = {}
    for o in srcs:
        for d in (o.writers, o.readers):
            for s, v in d.items():
                if merged.get(s, 0) < v:
                    merged[s] = v
    for o in srcs:
        o.writers = {}
        o.readers = {}
    for r in dsts:
        r.writers = {}
        r.readers = dict(merged)


class Prog:
    def __init__(self):
        self.q = {e: [] for e in ENGINES}
        self.cnt = {}
        self.guar = {e: {} for e in ENGINES}
        self.pend = {e: [] for e in ENGINES}

    def sem(self, name):
        self.cnt[name] = 0
        return name

    def _wait(self, eng, sname, val):
        if val <= 0:
            return
        if self.guar[eng].get(sname, 0) >= val:
            return
        self.guar[eng][sname] = val
        self.q[eng].append(("w", sname, val))

    def emit(self, eng, fn, reads=(), writes=(), inc=None, k=1):
        need = {}
        for r in reads:
            for s, v in r.writers.items():
                if need.get(s, 0) < v:
                    need[s] = v
        for r in writes:
            for d in (r.writers, r.readers):
                for s, v in d.items():
                    if need.get(s, 0) < v:
                        need[s] = v
        if inc is not None and inc[0] == "d":
            need.pop(inc, None)
        for s, v in need.items():
            self._wait(eng, s, v)
        if inc is None:
            self.q[eng].append(("op", fn, None, 0))
            self.pend[eng].append((tuple(reads), tuple(writes)))
            return None
        self.cnt[inc] += k
        gate = (inc, self.cnt[inc])
        if eng == "tensor" and inc == "T":
            self.guar[eng][inc] = self.cnt[inc]
        self.q[eng].append(("op", fn, inc, k))
        for rr, ww in self.pend[eng]:
            for r in rr:
                r.read(gate)
            for w in ww:
                w.wrote(gate)
        self.pend[eng].clear()
        for r in reads:
            r.read(gate)
        for r in writes:
            r.wrote(gate)
        return gate

    def raw_wait(self, eng, gatedict):
        for s, v in gatedict.items():
            self._wait(eng, s, v)


def build_nc():
    nc = bass.Bass("TRN2", target_bir_lowering=False, debug=False)
    P = Prog()
    for s in ("T", "V", "S", "G"):
        P.sem(s)
    DMA_SEMS = ("dxy", "dsl0", "dsl1", "dpf0", "dpf1", "dgw1", "doutw",
                "dw2a", "dw2b", "dl2a", "dl2b", "dmisc", "dbias",
                "dout0", "dout1", "dsb", "dsw2", "dsw2b", "dsl2s", "dsl2t")
    for s in DMA_SEMS:
        P.sem(s)

    dt_in = lambda n, sh, dt=F32: nc.dram_tensor(n, sh, dt, kind="ExternalInput").ap()
    x_tm = dt_in("x_tm", [S, D])
    x_fm = dt_in("x_fm", [D, S])
    pos_tm = dt_in("pos_tm", [S, D])
    pos_fm = dt_in("pos_fm", [D, S])
    gmem = dt_in("gmem", [64, D])
    convw = dt_in("convw", [4097, D])
    gate_w = dt_in("gate_w", [2048, D])
    out_w = dt_in("out_w", [D, D])
    w1f = dt_in("w1f", [D, 2048])
    w2f = dt_in("w2f", [2048, FF])
    lin2_w = dt_in("lin2_w", [FF, D])
    gate_b = dt_in("gate_b", [128, 8])
    out_b = dt_in("out_b", [128, 8])
    lin2_b = dt_in("lin2_b", [128, 8])
    router_w = dt_in("router_w", [128, 64])
    router_b = dt_in("router_b", [8, 1])
    out_fm = nc.dram_tensor("out_fm", [D, S], F32, kind="ExternalOutput").ap()
    dt_sc = lambda n, sh: nc.dram_tensor(n, sh, BF16, kind="Internal").ap()
    gw_bf = dt_sc("gw_bf", [2048, D])
    ow_bf = dt_sc("ow_bf", [D, D])
    w1_bf = dt_sc("w1_bf", [D, 2048])
    w2_bf = dt_sc("w2_bf", [2048, FF])
    l2_bf = dt_sc("l2_bf", [FF, D])

    import contextlib
    stack = contextlib.ExitStack()
    sb = lambda n, sh, dt: stack.enter_context(nc.sbuf_tensor(n, sh, dt)).ap()
    ps = lambda n, sh, dt: stack.enter_context(nc.psum_tensor(n, sh, dt)).ap()

    xy = sb("xy", [128, 8192], BF16)
    ynbf = sb("ynbf", [128, 8192], BF16)
    xpef = sb("xpef", [128, 4096], BF16)
    xpet = sb("xpet", [128, 4096], BF16)
    ah = sb("ah", [128, 16384], BF16)
    tbuf = sb("tbuf", [128, 8192], BF16)
    gw1a = sb("gw1a", [128, 16384], BF16)
    outwa = sb("outwa", [128, 8192], BF16)
    w2a = sb("w2a", [128, 16384], BF16)
    gt_tm = sb("gt_tm", [128, 1024], BF16)

    gt_fm = sb("gt_fm", [128, 1024], BF16)
    ebf = sb("ebf", [128, 512], F32)
    cfm = sb("cfm", [128, 512], BF16)
    ctm = sb("ctm", [128, 32], F32)
    csb = sb("csb", [128, 512], BF16)
    mwsb = sb("mwsb", [128, 512], BF16)
    l2a = sb("l2a", [128, 4096], BF16)
    ztm = sb("ztm", [128, 32], F32)
    rztm = sb("rztm", [128, 32], BF16)
    scr4 = sb("scr4", [128, 16], F32)
    scrf = sb("scrf", [128, 640], F32)
    outst = sb("outst", [128, 512], F32)
    statv = sb("statv", [1, 1536], F32)
    ones_b = sb("ones_b", [128, 128], BF16)
    ones_f = sb("ones_f", [128, 128], F32)
    id_f = sb("id_f", [128, 128], F32)
    id_b = sb("id_b", [128, 128], BF16)
    basis = sb("basis", [8, 1024], BF16)
    neg8 = sb("neg8", [128, 1], F32)
    epsc = sb("epsc", [128, 1], F32)
    gb_s = sb("gb_s", [128, 8], F32)
    ob_s = sb("ob_s", [128, 8], F32)
    lb_s = sb("lb_s", [128, 8], F32)
    rb_s = sb("rb_s", [8, 1], F32)
    rw_s = sb("rw_s", [128, 64], BF16)
    iot = sb("iot", [128, 1], F32)

    cbrow = l2a[0:1, 0:1024]
    pb = [ps(f"pb{i}", [128, 512], F32) for i in range(8)]
    pbb = [p.bitcast(BF16) for p in pb]

    LOC, GLO, MIX, PL0, PG0, RZ0 = 0, 4096, 8192, 12288, 14336, 15360

    R = lambda: Reg()
    Rxpef, Rxpet = R(), R()
    Rxyc = [R() for _ in range(NCH)]
    Rynbfc = [[R() for _ in range(NCH)] for _ in range(2)]
    Rloc, Rglo = R(), R()
    Rmixc = [R() for _ in range(NCH)]
    Rpl = [R() for _ in range(4)]
    Rpg = [R() for _ in range(2)]
    Rrz = [R() for _ in range(2)]
    Rah = R()
    Rtb = R()
    Rslab = [R() for _ in range(6)]
    Rgw1, Routw = R(), R()
    Rw2 = [R() for _ in range(2)]
    Rconv = [R() for _ in range(4)]
    Rl2 = [R() for _ in range(4)]
    Rout = [R() for _ in range(2)]
    Rgt_tm, Rgt_fm = R(), R()
    Rgb, Rob, Rlb, Rrb, Rrw, Rcbrow = R(), R(), R(), R(), R(), R()
    Rsgw, Rsow, Rsw1, Rsw2, Rsl2 = R(), R(), R(), R(), R()
    Rebf, Rcfm, Rctm, Rscr4, Rscrf, Rstat = R(), R(), R(), R(), R(), R()
    Rzt = [R(), R()]
    Rrzt = [R(), R()]
    Rcsb = R()
    Rmwsb = [R(), R()]
    Rb = [R() for _ in range(8)]
    Rconst = R()
    Rbias = R()


    def gp(fn, **kw):
        return P.emit("gpsimd", fn, **kw)

    def vx(fn, **kw):
        return P.emit("vector", fn, **kw)

    def sx(fn, **kw):
        return P.emit("scalar", fn, **kw)

    def tx(fn, **kw):
        return P.emit("tensor", fn, **kw)

    def dmaw(sem, fn, **kw):
        return P.emit("gpsimd", fn, inc=sem, k=16, **kw)

    def dmax(sem, fn, **kw):
        return P.emit("sync", fn, inc=sem, k=16, **kw)

    gp(lambda E: E.memset(ones_b[:], 1.0), writes=[Rconst], inc="G")
    gp(lambda E: E.memset(ones_f[:], 1.0), writes=[Rconst], inc="G")
    gp(lambda E: E.memset(neg8[:], CEXP), writes=[Rconst], inc="G")
    gp(lambda E: E.memset(epsc[:], 1e-5), writes=[Rconst], inc="G")
    gp(lambda E: E.iota(id_f[:], pattern=[[1, 128]], channel_multiplier=0,
                        allow_small_or_imprecise_dtypes=True),
       writes=[Rconst], inc="G")
    gp(lambda E: E.iota(iot[:], pattern=[[0, 1]], channel_multiplier=1,
                        allow_small_or_imprecise_dtypes=True),
       writes=[Rconst], inc="G")
    vx(lambda E: E.tensor_scalar(id_f[:], id_f[:], iot[:], None, ALU.is_equal),
       reads=[Rconst], writes=[Rconst], inc="V")
    vx(lambda E: E.tensor_copy(id_b[:], id_f[:]), reads=[Rconst], writes=[Rconst], inc="V")
    gp(lambda E: E.iota(basis[:], pattern=[[1, 8], [0, 128]], channel_multiplier=0,
                        allow_small_or_imprecise_dtypes=True),
       writes=[Rconst], inc="G")
    vx(lambda E: E.tensor_scalar(basis[:], basis[:], iot[0:8, :], None, ALU.is_equal),
       reads=[Rconst], writes=[Rconst], inc="V")
    dmax("dbias", lambda E: E.dma_start(out=gb_s[:], in_=gate_b), writes=[Rgb])
    dmax("dbias", lambda E: E.dma_start(out=ob_s[:], in_=out_b), writes=[Rob])
    dmax("dbias", lambda E: E.dma_start(out=lb_s[:], in_=lin2_b), writes=[Rlb])
    dmax("dbias", lambda E: E.dma_start(out=rb_s[:], in_=router_b), writes=[Rrb])
    dmaw("dmisc", lambda E: E.dma_start(out=rw_s[:], in_=router_w), writes=[Rrw])
    dmaw("dmisc", lambda E: E.dma_start(out=gt_tm[64:128, :], in_=gmem), writes=[Rgt_tm])
    dmaw("dmisc", lambda E: E.dma_start(out=cbrow[:], in_=convw[4096:4097, :]), writes=[Rcbrow])
    conv_src = convw[0:4096, :].rearrange("(j p) c -> p j c", p=128)

    def conv_group_load(g):
        slot = g % 2
        dst = w2a[:, slot * 4096:(slot + 1) * 4096].rearrange("p (j c) -> p j c", c=1024)
        dmaw("dw2a" if slot == 0 else "dw2b",
             lambda E, g=g, d=dst: E.dma_start(out=d, in_=conv_src[:, 4 * g:4 * g + 4, :]),
             writes=[Rconv[slot]])

    for r in (Rgb, Rob, Rlb, Rrb):
        r.writers = {"dbias": P.cnt["dbias"]}
    for r in (Rrw, Rgt_tm, Rcbrow):
        r.writers = {"dmisc": P.cnt["dmisc"]}

    conv_group_load(0)
    conv_group_load(1)
    gw_src = gate_w.rearrange("(k p) c -> p k c", p=128)
    w1_src = w1f.rearrange("(k p) c -> p k c", p=128)
    l2_src = lin2_w.rearrange("(k p) c -> p k c", p=128)
    w2_src = w2f.rearrange("(k p) (e c) -> p e k c", p=128, c=512)
    gwb_src = gw_bf.rearrange("(k p) c -> p k c", p=128)
    owb_src = ow_bf.rearrange("(k p) c -> p k c", p=128)
    w1b_src = w1_bf.rearrange("(k p) c -> p k c", p=128)
    l2b_src = l2_bf.rearrange("(k p) c -> p k c", p=128)
    w2b_src = w2_bf.rearrange("(k p) (e c) -> p e k c", p=128, c=512)

    def wb_store(sbuf_view, dst_view, slab_reg, scr_reg, sem="dsb"):
        P.emit("sync", lambda E, s=sbuf_view, d=dst_view: E.dma_start(out=d, in_=s),
               reads=[slab_reg], writes=[scr_reg], inc=sem, k=16)

    dmaw("dgw1", lambda E: E.dma_start(out=gw1a.rearrange("p (k c) -> p k c", c=1024),
                               in_=gw_src),
         writes=[Rgw1])
    wb_store(gw1a.rearrange("p (k c) -> p k c", c=1024), gwb_src, Rgw1, Rsgw)
    ow_src = out_w.rearrange("(k p) c -> p k c", p=128)
    dmaw("doutw", lambda E: E.dma_start(out=outwa.rearrange("p (k c) -> p k c", c=1024),
                                in_=ow_src),
         writes=[Routw])
    wb_store(outwa.rearrange("p (k c) -> p k c", c=1024), owb_src, Routw, Rsow)
    for r in (Rsgw, Rsow):
        r.writers = {"dsb": P.cnt["dsb"]}
    Rgw1.readers["dsb"] = P.cnt["dsb"]
    Routw.readers["dsb"] = P.cnt["dsb"]


    def ln_stage(sc, out_kind, t, bank_ids, scratch_reg):
        sc_base = sc * 512
        bsy, bsq, bb0, bb1 = bank_ids
        if out_kind == "ln1":
            srct, srcreg = xy, (lambda fc: Rxyc[fc])
        else:
            srct, srcreg = ynbf, (lambda fc: Rynbfc[sc][fc])
        for fc in range(NCH):
            xsl = srct[:, fc * 1024 + sc_base: fc * 1024 + sc_base + 512]
            pl = ah[:, PL0 + (fc % 2) * 512: PL0 + (fc % 2) * 512 + 512]
            srg = scratch_reg(fc)
            sx(lambda E, i=xsl, o=pl: E.activation(o, i, AF.Square),
               reads=[srcreg(fc)], writes=[srg], inc="S")
            tx(lambda E, r=xsl, fc=fc: E.matmul(pb[bsy][0:1, :], ones_b[:, 0:1], r,
                                                start=(fc == 0), stop=(fc == 7)),
               reads=[srcreg(fc), Rconst], writes=[Rb[bsy]], inc="T")
            tx(lambda E, r=pl, fc=fc: E.matmul(pb[bsq][0:1, :], ones_b[:, 0:1], r,
                                               start=(fc == 0), stop=(fc == 7)),
               reads=[srg, Rconst], writes=[Rb[bsq]], inc="T")
        mean = statv[0:1, 0:512]
        inv = statv[0:1, 512:1024]
        vx(lambda E: E.tensor_scalar(mean, pb[bsy][0:1, :], 1.0 / 1024, None, ALU.mult),
           reads=[Rb[bsy]], writes=[Rstat], inc="V")
        vx(lambda E: E.tensor_mul(inv, mean, mean), reads=[Rstat], writes=[Rstat],
           inc="V")
        vx(lambda E: E.scalar_tensor_tensor(inv, pb[bsq][0:1, :], 1.0 / 1024, inv,
                                            ALU.mult, ALU.subtract),
           reads=[Rb[bsq], Rstat], writes=[Rstat], inc="V")
        sx(lambda E: E.activation(inv, inv, AF.Ln, bias=epsc[0:1, :]),
           reads=[Rstat, Rconst], writes=[Rstat], inc="S")
        sx(lambda E: E.activation(inv, inv, AF.Exp, scale=-0.5),
           reads=[Rstat], writes=[Rstat], inc="S")
        invb = statv.bitcast(BF16)[0:1, 2048:2560]
        minvb = statv.bitcast(BF16)[0:1, 2560:3072]
        vx(lambda E: E.tensor_copy(invb, inv), reads=[Rstat], writes=[Rstat], inc="V")
        vx(lambda E: E.tensor_mul(mean, mean, inv), reads=[Rstat], writes=[Rstat],
           inc="V")
        vx(lambda E: E.tensor_copy(minvb, mean), reads=[Rstat], writes=[Rstat],
           inc="V")
        tx(lambda E: E.matmul(pb[bb0][:], ones_b[0:1, :], invb, start=True, stop=True),
           reads=[Rstat, Rconst], writes=[Rb[bb0]], inc="T")
        tx(lambda E: E.matmul(pb[bb1][:], ones_b[0:1, :], minvb, start=True, stop=True),
           reads=[Rstat, Rconst], writes=[Rb[bb1]], inc="T")
        for fc in range(NCH):
            xsl = srct[:, fc * 1024 + sc_base: fc * 1024 + sc_base + 512]
            q1 = scrf[:, 0:512]
            vx(lambda E, i=xsl, o=q1: E.tensor_mul(o, i, pb[bb0][:]),
               reads=[srcreg(fc), Rb[bb0]], writes=[Rscrf], inc="V")
            if out_kind == "ln1":
                osl = ynbf[:, fc * 1024 + sc_base: fc * 1024 + sc_base + 512]
                vx(lambda E, i=q1, o=osl: E.tensor_sub(o, i, pb[bb1][:]),
                   reads=[Rscrf, Rb[bb1]], writes=[Rynbfc[sc][fc]], inc="V")
            else:
                slot = 0
                osl = outst[:, 0:512]
                vx(lambda E, i=q1, o=osl: E.tensor_sub(o, i, pb[bb1][:]),
                   reads=[Rscrf, Rb[bb1]], writes=[Rout[slot]], inc="V")
                dst = out_fm[fc * 128:(fc + 1) * 128,
                             t * 1024 + sc_base: t * 1024 + sc_base + 512]
                P.emit("sync", lambda E, s=osl, d=dst: E.dma_start(out=d, in_=s),
                       reads=[Rout[slot]], inc="dout%d" % slot, k=16)

    for t in range(NT):
        tb0 = t * 1024
        if t > 0:
            alias_edge(Rpl + Rpg + Rrz + [Rloc, Rglo] + Rmixc, [Rah])
        xfm_g = x_fm.rearrange("(f p) s -> p f s", p=128)
        pfm_g = pos_fm.rearrange("(f p) s -> p f s", p=128)
        xtm_g = x_tm.rearrange("(g p) d -> p g d", p=128)
        ptm_g = pos_tm.rearrange("(g p) d -> p g d", p=128)
        def emit_A(tt_, sc_):
            base = tt_ * 1024 + sc_ * 512
            if tt_ >= NT:
                return
            dmaw("dpf0", lambda E, s=xfm_g[:, :, base: base + 512],
                 d=xpef.rearrange("p (f s) -> p f s", s=512): E.dma_start(
                     out=d, in_=s),
                 writes=[Rxpef])
            dmaw("dpf1", lambda E, s=pfm_g[:, :, base: base + 512],
                 d=xpef.rearrange("p (f s) -> p f s", s=512): E.dma_start(
                     out=d, in_=s, accum_op=ALU.add),
                 writes=[Rxpef])
            g0 = base // 128
            dmaw("dsl0", lambda E, s=xtm_g[:, g0:g0 + 4, :],
                 d=xpet.rearrange("p (g d) -> p g d", d=1024): E.dma_start(
                     out=d, in_=s),
                 writes=[Rxpet])
            dmaw("dsl1", lambda E, s=ptm_g[:, g0:g0 + 4, :],
                 d=xpet.rearrange("p (g d) -> p g d", d=1024): E.dma_start(
                     out=d, in_=s, accum_op=ALU.add),
                 writes=[Rxpet])

        for sc in range(2):
            scb = sc * 512
            if t == 0 and sc == 0:
                emit_A(0, 0)
            if sc == 0:
                for fcx in range(NCH):
                    srcx = x_fm[fcx * 128:(fcx + 1) * 128, tb0:tb0 + 1024]
                    dstx = xy[:, fcx * 1024: fcx * 1024 + 1024]
                    dmaw("dxy", lambda E, s=srcx, d=dstx: E.dma_start(out=d, in_=s),
                         writes=[Rxyc[fcx]])
                for rch in Rxyc:
                    rch.writers = {"dxy": P.cnt["dxy"]}

            scnt = 0
            pcnt = 0
            for q in range(4):
                qs = q % 2
                for hp in range(2):
                    hA = q * 4 + hp * 2
                    fc = hA // 2
                    pair0 = pcnt
                    def emit_av(hh2, pbase, kc):
                        h = q * 4 + hh2
                        av_bank = 3 + hh2
                        slot = (pair0 + kc * 2 + (hh2 - hp * 2)) % 4
                        pls = ah[:, PL0 + slot * 512: PL0 + slot * 512 + 512]
                        lhs = xpet[:, kc * 1024 + 64 * h:
                                   kc * 1024 + 64 * h + 64]
                        out = pb[av_bank][pbase:pbase + 64, :]
                        tx(lambda E, l=lhs, r=pls, o=out, kc=kc: E.matmul(
                            o, l, r, start=(kc == 0), stop=(kc == 3)),
                           reads=[Rpl[slot], Rxpet], writes=[Rb[av_bank]],
                           inc=("T" if kc == 3 else None))

                    for kc in range(4):
                        for hh2, pbase in ((hp * 2, 0), (hp * 2 + 1, 64)):
                            h = q * 4 + hh2
                            b = scnt % 3
                            scnt += 1
                            lhs = xpef[pbase:pbase + 64,
                                       fc * 512 + kc * 128:
                                       fc * 512 + kc * 128 + 128]
                            rhs = xpef[pbase:pbase + 64,
                                       fc * 512: fc * 512 + 512]
                            tx(lambda E, l=lhs, r=rhs, b=b: E.matmul(
                                pb[b][:], l, r, start=True, stop=True),
                               reads=[Rxpef], writes=[Rb[b]], inc="T")
                            slot = pcnt % 4
                            pcnt += 1
                            pls = ah[:, PL0 + slot * 512: PL0 + slot * 512 + 512]
                            zc = ztm[:, qs * 16 + hh2 * 4 + kc:
                                     qs * 16 + hh2 * 4 + kc + 1]
                            sx(lambda E, b=b, o=pls, z=zc: E.activation(
                                o, pb[b][:], AF.Exp, bias=neg8[:], scale=SCALE,
                                accum_out=z),
                               reads=[Rb[b], Rconst], writes=[Rpl[slot], Rzt[qs]],
                               inc="S")
                        if kc >= 1:
                            emit_av(hp * 2, 0, kc - 1)
                            emit_av(hp * 2 + 1, 64, kc - 1)
                    emit_av(hp * 2, 0, 3)
                    emit_av(hp * 2 + 1, 64, 3)
                vx(lambda E, qs=qs: E.reciprocal(rztm[:, qs * 16: qs * 16 + 16],
                                                 ztm[:, qs * 16: qs * 16 + 16]),
                   reads=[Rzt[qs]], writes=[Rrzt[qs]], inc="V")
                for hh in range(4):
                    h = q * 4 + hh
                    fc, pbase = h // 2, 64 * (h % 2)
                    hbase = (h % 2) * 64
                    for kc in range(4):
                        rc = rztm[:, qs * 16 + hh * 4 + kc:
                                  qs * 16 + hh * 4 + kc + 1].broadcast_to((128, 64))
                        out = pb[7][hbase:hbase + 64, kc * 128:kc * 128 + 128]
                        tx(lambda E, i=rc, o=out, kc=kc: E.matmul(
                            o, i, id_b[:], start=(kc == 0), stop=(kc == 3)),
                           reads=[Rrzt[qs], Rconst], writes=[Rb[7]],
                           inc=("T" if kc == 3 else None))
                    if hh % 2 == 1:
                        vx(lambda E: E.tensor_copy(csb[:], pb[7][:]),
                           reads=[Rb[7]], writes=[Rcsb], inc="V")
                        avA, avB = 3 + hh - 1, 3 + hh
                        lA = ah[0:64, LOC + fc * 512: LOC + fc * 512 + 512]
                        lB = ah[64:128, LOC + fc * 512: LOC + fc * 512 + 512]
                        vx(lambda E, o=lA, ab=avA: E.tensor_mul(
                            o, pb[ab][0:64, :], csb[0:64, :]),
                           reads=[Rb[avA], Rcsb], writes=[Rloc], inc="V")
                        vx(lambda E, o=lB, ab=avB: E.tensor_mul(
                            o, pb[ab][64:128, :], csb[64:128, :]),
                           reads=[Rb[avB], Rcsb], writes=[Rloc], inc="V")

            if t == 0 and sc == 0:
                for j in range(32):
                    kc, ic = j // 8, j % 8
                    g, jj = j // 4, j % 4
                    slot = g % 2
                    if jj == 0 and g >= 2:
                        conv_group_load(g)
                    lhs = xpef[:, ic * 512: ic * 512 + 256].rearrange(
                        "p (t f) -> p t f", f=4)[:, :, kc]
                    for oh in range(2):
                        bank = 3 + oh
                        rhs = w2a[:, slot * 4096 + jj * 1024 + oh * 512:
                                  slot * 4096 + jj * 1024 + oh * 512 + 512]
                        tx(lambda E, l=lhs, r=rhs, b=bank, j=j: E.matmul(
                            pb[b][0:64, :], l, r, start=(j == 0), stop=False),
                           reads=[Rxpef, Rconv[slot]], writes=[Rb[bank]],
                           inc=("T" if (j % 4) == 3 and oh == 1 else None))
                for oh in range(2):
                    bank = 3 + oh
                    rhsb = cbrow[0:1, oh * 512: oh * 512 + 512]
                    tx(lambda E, r=rhsb, b=bank: E.matmul(
                        pb[b][0:64, :], ones_b[0:1, 0:64], r, start=False, stop=True),
                       reads=[Rcbrow, Rconst], writes=[Rb[bank]], inc="T")
                    vx(lambda E, b=bank, oh=oh: E.tensor_copy(
                        gt_tm[0:64, oh * 512: oh * 512 + 512], pb[b][0:64, :]),
                       reads=[Rb[bank]], writes=[Rgt_tm], inc="V")
                for fc in range(NCH):
                    bank = 5
                    tin = gt_tm[:, fc * 128: fc * 128 + 128]
                    tx(lambda E, i=tin, b=bank: E.matmul(
                        pbb[b][:, 0:128], i, id_b[:], start=True, stop=True,
                        is_transpose=True),
                       reads=[Rgt_tm, Rconst], writes=[Rb[bank]], inc="T")
                    vx(lambda E, fc=fc, b=bank: E.tensor_copy(
                        gt_fm[:, fc * 128: fc * 128 + 128], pbb[b][:, 0:128]),
                       reads=[Rb[bank]], writes=[Rgt_fm], inc="V")

            scnt = 0
            for q in range(4):
                qs = q % 2
                for hh in range(4):
                    h = q * 4 + hh
                    fc, pbase = h // 2, 64 * (h % 2)
                    b = scnt % 2
                    scnt += 1
                    lhs = gt_fm[pbase:pbase + 64, fc * 128: fc * 128 + 128]
                    rhs = xpef[pbase:pbase + 64, fc * 512: fc * 512 + 512]
                    tx(lambda E, l=lhs, r=rhs, b=b: E.matmul(
                        pb[b][:], l, r, start=True, stop=True),
                       reads=[Rgt_fm, Rxpef], writes=[Rb[b]], inc="T")
                    slot = h % 2
                    pgs = ah[:, PG0 + slot * 512: PG0 + slot * 512 + 512]
                    sx(lambda E, b=b, o=pgs: E.activation(o, pb[b][:], AF.Exp,
                                                          bias=neg8[:], scale=SCALE),
                       reads=[Rb[b], Rconst], writes=[Rpg[slot]], inc="S")
                    for tt in range(4):
                        zout = pb[2][:, qs * 16 + hh * 4 + tt: qs * 16 + hh * 4 + tt + 1]
                        tx(lambda E, l=pgs, o=zout, tt=tt, hh=hh, qs=qs: E.matmul(
                            o, l[:, tt * 128:tt * 128 + 128], ones_b[:, 0:1],
                            start=(hh == 0 and tt == 0), stop=(hh == 3 and tt == 3)),
                           reads=[Rpg[slot], Rconst], writes=[Rb[2]],
                           inc=("T" if (hh == 3 and tt == 3) else None))
                    av_bank = 3 + hh
                    lhs2 = gt_tm[:, h * 64: h * 64 + 64]
                    out = pb[av_bank][pbase:pbase + 64, :]
                    tx(lambda E, l=lhs2, r=pgs, o=out: E.matmul(
                        o, l, r, start=True, stop=True),
                       reads=[Rpg[slot], Rgt_tm], writes=[Rb[av_bank]], inc="T")
                vx(lambda E, qs=qs: E.reciprocal(rztm[:, qs * 16: qs * 16 + 16],
                                                 pb[2][:, qs * 16: qs * 16 + 16]),
                   reads=[Rb[2]], writes=[Rrzt[qs]], inc="V")
                for hh in range(4):
                    h = q * 4 + hh
                    fc, pbase = h // 2, 64 * (h % 2)
                    hbase = (h % 2) * 64
                    for kc in range(4):
                        rc = rztm[:, qs * 16 + hh * 4 + kc:
                                  qs * 16 + hh * 4 + kc + 1].broadcast_to((128, 64))
                        out = pb[7][hbase:hbase + 64, kc * 128:kc * 128 + 128]
                        tx(lambda E, i=rc, o=out, kc=kc: E.matmul(
                            o, i, id_b[:], start=(kc == 0), stop=(kc == 3)),
                           reads=[Rrzt[qs], Rconst], writes=[Rb[7]],
                           inc=("T" if kc == 3 else None))
                    if hh % 2 == 1:
                        vx(lambda E: E.tensor_copy(csb[:], pb[7][:]),
                           reads=[Rb[7]], writes=[Rcsb], inc="V")
                        avA, avB = 3 + hh - 1, 3 + hh
                        gA = ah[0:64, GLO + fc * 512: GLO + fc * 512 + 512]
                        gB = ah[64:128, GLO + fc * 512: GLO + fc * 512 + 512]
                        vx(lambda E, o=gA, ab=avA: E.tensor_mul(
                            o, pb[ab][0:64, :], csb[0:64, :]),
                           reads=[Rb[avA], Rcsb], writes=[Rglo], inc="V")
                        vx(lambda E, o=gB, ab=avB: E.tensor_mul(
                            o, pb[ab][64:128, :], csb[64:128, :]),
                           reads=[Rb[avB], Rcsb], writes=[Rglo], inc="V")

            emit_A(t + (sc == 1), (sc + 1) % 2)

            for m in range(NCH):
                bank = m % 3
                for k in range(16):
                    lhs = gw1a[:, k * 1024 + m * 128: k * 1024 + m * 128 + 128]
                    base = LOC if k < 8 else GLO
                    rr = Rloc if k < 8 else Rglo
                    rhs = ah[:, base + (k % 8) * 512: base + (k % 8) * 512 + 512]
                    tx(lambda E, l=lhs, r=rhs, b=bank, k=k: E.matmul(
                        pb[b][:], l, r, start=(k == 0), stop=(k == 15)),
                       reads=[Rgw1, rr], writes=[Rb[bank]],
                       inc=("T" if k == 15 else None))
                msl = ah[:, MIX + m * 512: MIX + m * 512 + 512]
                sx(lambda E, b=bank, o=msl, m=m: E.activation(
                    o, pb[b][:], AF.Sigmoid, bias=gb_s[:, m:m + 1]),
                   reads=[Rb[bank], Rgb], writes=[Rmixc[m]], inc="S")
                lsl = ah[:, LOC + m * 512: LOC + m * 512 + 512]
                gsl = ah[:, GLO + m * 512: GLO + m * 512 + 512]
                t1 = ah[:, PL0 + (m % 4) * 512: PL0 + (m % 4) * 512 + 512]
                gp(lambda E, a=lsl, b=gsl, o=t1: E.tensor_sub(o, a, b),
                   reads=[Rloc, Rglo], writes=[Rpl[m % 4]], inc="G")
                vx(lambda E, a=t1, b=msl, o=t1: E.tensor_mul(o, a, b),
                   reads=[Rpl[m % 4], Rmixc[m]], writes=[Rpl[m % 4]], inc="V")
                vx(lambda E, a=t1, b=gsl, o=msl: E.tensor_add(o, a, b),
                   reads=[Rpl[m % 4], Rglo], writes=[Rmixc[m]], inc="V")
            for m in range(NCH):
                bank = 3 + (m % 3)
                for k in range(NCH):
                    lhs = outwa[:, k * 1024 + m * 128: k * 1024 + m * 128 + 128]
                    rhs = ah[:, MIX + k * 512: MIX + k * 512 + 512]
                    tx(lambda E, l=lhs, r=rhs, b=bank, k=k: E.matmul(
                        pb[b][:], l, r, start=(k == 0), stop=(k == 7)),
                       reads=[Routw, Rmixc[k]], writes=[Rb[bank]],
                       inc=("T" if k == 7 else None))
                xsl = xy[:, m * 1024 + scb: m * 1024 + scb + 512]
                vx(lambda E, b=bank, o=xsl, m=m: E.scalar_tensor_tensor(
                    o, pb[b][:], ob_s[:, m:m + 1], o, ALU.add, ALU.add),
                   reads=[Rb[bank], Rob, Rxyc[m]], writes=[Rxyc[m]], inc="V")
            ln_stage(sc, "ln1", t, (0, 1, 6, 7), lambda fc: Rpl[fc % 2])

        alias_edge([Rah], Rpl + Rpg + Rrz + [Rloc, Rglo] + Rmixc)
        if t == 0:
            dmaw("dgw1", lambda E: E.dma_start(
                out=gw1a.rearrange("p (k c) -> p k c", c=2048), in_=w1_src),
                 writes=[Rgw1])
            wb_store(gw1a.rearrange("p (k c) -> p k c", c=2048), w1b_src, Rgw1, Rsw1)
        else:
            P.emit("gpsimd", lambda E: E.dma_start(
                out=gw1a.rearrange("p (k c) -> p k c", c=2048), in_=w1b_src),
                reads=[Rsw1], writes=[Rgw1], inc="dgw1", k=16)
        if t == 0:
            alias_edge(Rw2, Rconv)

        for sc in range(2):
            scb = sc * 512
            bank_lg = 2
            for fc in range(NCH):
                lhs = rw_s[:, fc * 8: fc * 8 + 8]
                rhs = ynbf[:, fc * 1024 + scb: fc * 1024 + scb + 512]
                tx(lambda E, l=lhs, r=rhs, fc=fc: E.matmul(
                    pb[bank_lg][0:8, :], l, r, start=(fc == 0), stop=(fc == 7)),
                   reads=[Rrw, Rynbfc[sc][fc]], writes=[Rb[bank_lg]],
                   inc=("T" if fc == 7 else None))
            sx(lambda E: E.activation(ebf[0:8, :], pb[bank_lg][0:8, :], AF.Exp,
                                      bias=rb_s[:]),
               reads=[Rb[bank_lg], Rrb], writes=[Rebf], inc="S")
            bank_et = 3
            for u in range(4):
                tin = ebf[0:8, u * 128: u * 128 + 128]
                out = pb[bank_et][:, u * 8: u * 8 + 8]
                tx(lambda E, i=tin, o=out, u=u: E.matmul(
                    o, i, id_f[0:8, 0:8], start=(u == 0), stop=(u == 3),
                    is_transpose=True),
                   reads=[Rebf, Rconst], writes=[Rb[bank_et]],
                   inc=("T" if u == 3 else None))
            et = pb[bank_et][:, 0:32].rearrange("p (a b) -> p a b", b=8)
            zt = scr4[:, 8:12]
            rzt = scr4[:, 12:16]
            m1 = scr4[:, 0:4]
            m2 = scr4[:, 4:8]
            eq = scrf[:, 512:544]
            dd = scrf[:, 544:576]
            vx(lambda E: E.reduce_sum(zt, et, axis=AX.X), reads=[Rb[bank_et]],
               writes=[Rscr4], inc="V")
            vx(lambda E: E.reciprocal(rzt, zt), reads=[Rscr4], writes=[Rscr4], inc="V")
            vx(lambda E: E.reduce_max(m1, et, axis=AX.X), reads=[Rb[bank_et]],
               writes=[Rscr4], inc="V")
            m1b = m1.rearrange("p (a b) -> p a b", b=1).broadcast_to((128, 4, 8))
            vx(lambda E: E.tensor_tensor(eq.rearrange("p (a b) -> p a b", b=8), et, m1b,
                                         ALU.is_equal),
               reads=[Rb[bank_et], Rscr4], writes=[Rscrf], inc="V")
            vx(lambda E: E.scalar_tensor_tensor(dd, eq, -1e9, et.rearrange("p a b -> p (a b)"),
                                                ALU.mult, ALU.add),
               reads=[Rscrf, Rb[bank_et]], writes=[Rscrf], inc="V")
            vx(lambda E: E.reduce_max(m2, dd.rearrange("p (a b) -> p a b", b=8), axis=AX.X),
               reads=[Rscrf], writes=[Rscr4], inc="V")
            m2b = m2.rearrange("p (a b) -> p a b", b=1).broadcast_to((128, 4, 8))
            vx(lambda E: E.tensor_tensor(eq.rearrange("p (a b) -> p a b", b=8), et, m2b,
                                         ALU.is_ge),
               reads=[Rb[bank_et], Rscr4], writes=[Rscrf], inc="V")
            vx(lambda E: E.tensor_tensor(dd.rearrange("p (a b) -> p a b", b=8),
                                         eq.rearrange("p (a b) -> p a b", b=8), et,
                                         ALU.mult),
               reads=[Rscrf, Rb[bank_et]], writes=[Rscrf], inc="V")
            rztb = rzt.rearrange("p (a b) -> p a b", b=1).broadcast_to((128, 4, 8))
            vx(lambda E: E.tensor_tensor(ctm.rearrange("p (a b) -> p a b", b=8),
                                         dd.rearrange("p (a b) -> p a b", b=8), rztb,
                                         ALU.mult),
               reads=[Rscrf, Rscr4], writes=[Rctm], inc="V")
            bank_cf = 4
            for u in range(4):
                tin = ctm[:, u * 8: u * 8 + 8]
                out = pb[bank_cf][0:8, u * 128: u * 128 + 128]
                tx(lambda E, i=tin, o=out, u=u: E.matmul(
                    o, i, id_f[:], start=(u == 0), stop=(u == 3),
                    is_transpose=True),
                   reads=[Rctm, Rconst], writes=[Rb[bank_cf]],
                   inc=("T" if u == 3 else None))
            vx(lambda E: E.tensor_copy(cfm[0:8, :], pb[bank_cf][0:8, 0:512]),
               reads=[Rb[bank_cf]], writes=[Rcfm], inc="V")

            for er in range(16):
                e = er // 2
                mslot = 0
                msb = mwsb[:, 0:512]
                if er % 2 == 0:
                    mwb_bank = 5 + mslot
                    lhs = basis[0:8, e * 128: e * 128 + 128]
                    tx(lambda E, l=lhs, b=mwb_bank: E.matmul(pb[b][:], l, cfm[0:8, :],
                                                             start=True, stop=True),
                       reads=[Rcfm, Rconst], writes=[Rb[mwb_bank]], inc="T")
                    sx(lambda E, b=mwb_bank, o=msb: E.copy(o, pb[b][:]),
                       reads=[Rb[mwb_bank]], writes=[Rmwsb[mslot]], inc="S")
                bank = er % 3
                for k in range(NCH):
                    lhs = gw1a[:, k * 2048 + er * 128: k * 2048 + er * 128 + 128]
                    rhs = ynbf[:, k * 1024 + scb: k * 1024 + scb + 512]
                    tx(lambda E, l=lhs, r=rhs, b=bank, k=k: E.matmul(
                        pb[b][:], l, r, start=(k == 0), stop=(k == 7)),
                       reads=[Rgw1, Rynbfc[sc][k]], writes=[Rb[bank]],
                       inc=("T" if k == 7 else None))
                tsl = tbuf[:, er * 512: er * 512 + 512]
                vx(lambda E, b=bank, o=tsl, m=msb: E.tensor_mul(o, pb[b][:], m),
                   reads=[Rb[bank], Rmwsb[mslot]], writes=[Rtb], inc="V")

            for e8 in range(8):
                reg = e8 % 2
                dst = w2a[:, reg * 8192: reg * 8192 + 8192].rearrange(
                    "p (k c) -> p k c", c=512)
                if t == 0 and sc == 0:
                    dmaw("dw2a" if reg == 0 else "dw2b",
                         lambda E, d=dst, e8=e8: E.dma_start(out=d, in_=w2_src[:, e8]),
                         writes=[Rw2[reg]])
                    wb_store(dst, w2b_src[:, e8], Rw2[reg], Rsw2,
                             "dsw2" if reg == 0 else "dsw2b")
                else:
                    P.emit("gpsimd",
                           lambda E, d=dst, e8=e8: E.dma_start(out=d, in_=w2b_src[:, e8]),
                           reads=[Rsw2], writes=[Rw2[reg]],
                           inc=("dw2a" if reg == 0 else "dw2b"), k=16)
                for f4 in range(4):
                    f = e8 * 4 + f4
                    bank = f % 3
                    for k in range(16):
                        lhs = w2a[:, reg * 8192 + k * 512 + f4 * 128:
                                  reg * 8192 + k * 512 + f4 * 128 + 128]
                        rhs = tbuf[:, k * 512: k * 512 + 512]
                        tx(lambda E, l=lhs, r=rhs, b=bank, k=k: E.matmul(
                            pb[b][:], l, r, start=(k == 0), stop=(k == 15)),
                           reads=[Rw2[reg], Rtb], writes=[Rb[bank]],
                           inc=("T" if k == 15 else None))
                    hsl = ah[:, f * 512: f * 512 + 512]
                    sx(lambda E, b=bank, o=hsl: E.activation(o, pb[b][:], AF.Gelu),
                       reads=[Rb[bank]], writes=[Rah], inc="S")

            if t == 0 and sc == 0:
                Rsw2.writers = {"dsw2": P.cnt["dsw2"], "dsw2b": P.cnt["dsw2b"]}
            if t == 0 and sc == 0:
                alias_edge([Rl2[0]], [Rcbrow])
            for kf in range(32):
                slot = (kf // 2) % 2
                if kf % 2 == 0:
                    dst = l2a[:, slot * 2048: slot * 2048 + 2048].rearrange(
                        "p (k c) -> p k c", c=1024)
                    if t == 0 and sc == 0:
                        dmaw("dl2a" if slot == 0 else "dl2b",
                             lambda E, d=dst, kf=kf: E.dma_start(
                                 out=d, in_=l2_src[:, kf:kf + 2, :]),
                             writes=[Rl2[slot]])
                        wb_store(dst, l2b_src[:, kf:kf + 2, :], Rl2[slot], Rsl2,
                                 "dsl2s" if slot == 0 else "dsl2t")
                    else:
                        P.emit("gpsimd",
                               lambda E, d=dst, kf=kf: E.dma_start(
                                   out=d, in_=l2b_src[:, kf:kf + 2, :]),
                               reads=[Rsl2], writes=[Rl2[slot]],
                               inc=("dl2a" if slot == 0 else "dl2b"), k=16)
                off = (kf % 2) * 1024
                for m in range(NCH):
                    lhs = l2a[:, slot * 2048 + off + m * 128:
                              slot * 2048 + off + m * 128 + 128]
                    rhs = ah[:, kf * 512: kf * 512 + 512]
                    tx(lambda E, l=lhs, r=rhs, b=m, kf=kf: E.matmul(
                        pb[b][:], l, r, start=(kf == 0), stop=(kf == 31)),
                       reads=[Rl2[slot], Rah], writes=[Rb[m]],
                       inc=("T" if (kf % 2 == 1 and m == NCH - 1) else None))
            if t == 0 and sc == 0:
                Rsl2.writers = {"dsl2s": P.cnt["dsl2s"], "dsl2t": P.cnt["dsl2t"]}
            for m in range(NCH):
                ysl = ynbf[:, m * 1024 + scb: m * 1024 + scb + 512]
                vx(lambda E, b=m, y=ysl, m=m: E.scalar_tensor_tensor(
                    y, pb[b][:], lb_s[:, m:m + 1], y, ALU.add, ALU.add),
                   reads=[Rb[m], Rlb, Rynbfc[sc][m]], writes=[Rynbfc[sc][m]],
                   inc="V")
            ln_stage(sc, "ln2", t, (0, 1, 6, 7), lambda fc: Rah)

        if t + 1 < NT:
            P.emit("gpsimd", lambda E: E.dma_start(
                out=gw1a.rearrange("p (k c) -> p k c", c=1024), in_=gwb_src),
                reads=[Rsgw], writes=[Rgw1], inc="dgw1", k=16)
            P.emit("gpsimd", lambda E: E.dma_start(
                out=outwa.rearrange("p (k c) -> p k c", c=1024), in_=owb_src),
                reads=[Rsow], writes=[Routw], inc="doutw", k=16)

    P.raw_wait("sync", {"dout0": P.cnt["dout0"], "dout1": P.cnt["dout1"]})

    semh = {}
    for name in ("T", "V", "S", "G") + DMA_SEMS:
        semh[name] = stack.enter_context(nc.semaphore(name))

    ctx_lp = nc.allow_low_precision(
        reason="bf16 operands for matmuls are intended; fp32 PSUM accumulation")
    ctx_lp.__enter__()
    with nc.Block() as block:
        def run_queue(qname):
            def runner(E):
                for item in P.q[qname]:
                    if item[0] == "w":
                        E.wait_ge(semh[item[1]], item[2])
                    else:
                        _, fn, inc, k = item
                        ins = fn(E)
                        if inc is not None:
                            ins.then_inc(semh[inc], k)
            return runner

        block.tensor(run_queue("tensor"))
        block.vector(run_queue("vector"))
        block.scalar(run_queue("scalar"))
        block.sync(run_queue("sync"))
        block.gpsimd(run_queue("gpsimd"))

    ctx_lp.__exit__(None, None, None)
    stack.close()
    return nc


_NC_CACHE = None


def kernel(L0_attn_x, L0_attn_pos_emb, L0_attn_global_memory, L0_attn_conv_w,
           L0_attn_conv_b, L0_attn_gate_w, L0_attn_gate_b, L0_attn_out_w,
           L0_attn_out_b, L0_ffn_router_w, L0_ffn_router_b, L0_ffn_w1,
           L0_ffn_w2, L0_ffn_lin2_w, L0_ffn_lin2_b):
    global _NC_CACHE
    if _NC_CACHE is None:
        _NC_CACHE = build_nc()
    nc = _NC_CACHE

    f32 = np.float32
    ac = np.ascontiguousarray
    pos = ac(L0_attn_pos_emb[:S], dtype=f32)
    pos_f = ac(pos.T)
    convw = np.concatenate(
        [L0_attn_conv_w.transpose(2, 1, 0).reshape(4096, D),
         L0_attn_conv_b.reshape(1, D)], axis=0)
    convw = ac(convw, dtype=f32)
    gate_w = ac(L0_attn_gate_w, dtype=f32)
    out_w = ac(L0_attn_out_w, dtype=f32)
    w1fh = ac(L0_ffn_w1.transpose(1, 0, 2).reshape(D, 2048), dtype=f32)
    w2fh = ac(L0_ffn_w2.reshape(2048, FF), dtype=f32)
    lin2 = ac(L0_ffn_lin2_w, dtype=f32)
    gb = ac(L0_attn_gate_b.reshape(8, 128).T, dtype=f32)
    ob = ac(L0_attn_out_b.reshape(8, 128).T, dtype=f32)
    lb = ac(L0_ffn_lin2_b.reshape(8, 128).T, dtype=f32)
    rw = ac(L0_ffn_router_w.reshape(8, 128, 8).transpose(1, 0, 2).reshape(128, 64),
            dtype=f32)
    rb = ac(L0_ffn_router_b.reshape(8, 1), dtype=f32)
    gm = ac(L0_attn_global_memory[0], dtype=f32)

    in_maps = []
    for b in range(8):
        xb = ac(L0_attn_x[b], dtype=f32)
        in_maps.append({
            "x_tm": xb, "x_fm": ac(xb.T),
            "pos_tm": pos, "pos_fm": pos_f,
            "gmem": gm, "convw": convw,
            "gate_w": gate_w, "out_w": out_w,
            "w1f": w1fh, "w2f": w2fh, "lin2_w": lin2,
            "gate_b": gb, "out_b": ob, "lin2_b": lb,
            "router_w": rw, "router_b": rb,
        })
    res = run_bass_kernel_spmd(nc, in_maps, core_ids=list(range(8)))
    out = np.empty((8, S, D), np.float32)
    for b in range(8):
        out[b] = res.results[b]["out_fm"].T
    return out


# version 16
# speedup vs baseline: 1.1511x; 1.0367x over previous
import numpy as np
import concourse.bass as bass
import concourse.mybir as mybir
from concourse.bass_utils import run_bass_kernel_spmd

F32 = mybir.dt.float32
BF16 = mybir.dt.bfloat16
AF = mybir.ActivationFunctionType
ALU = mybir.AluOpType
AX = mybir.AxisListType

S, D, H, HD, WIN = 4096, 1024, 16, 64, 512
TILE, NT, SC = 1024, 4, 512
NCH = 8
E8, RR, FF = 8, 256, 4096
SCALE = 0.125
CEXP = -8.0

ENGINES = ("tensor", "vector", "scalar", "sync", "gpsimd")


class Reg:
    __slots__ = ("writers", "readers")

    def __init__(self):
        self.writers = {}
        self.readers = {}

    def wrote(self, gate):
        s, v = gate
        self.writers = {s: v}
        self.readers = {}

    def read(self, gate):
        s, v = gate
        if self.readers.get(s, 0) < v:
            self.readers[s] = v

def alias_edge(dsts, srcs):
    merged = {}
    for o in srcs:
        for d in (o.writers, o.readers):
            for s, v in d.items():
                if merged.get(s, 0) < v:
                    merged[s] = v
    for o in srcs:
        o.writers = {}
        o.readers = {}
    for r in dsts:
        r.writers = {}
        r.readers = dict(merged)


class Prog:
    def __init__(self):
        self.q = {e: [] for e in ENGINES}
        self.cnt = {}
        self.guar = {e: {} for e in ENGINES}
        self.pend = {e: [] for e in ENGINES}

    def sem(self, name):
        self.cnt[name] = 0
        return name

    def _wait(self, eng, sname, val):
        if val <= 0:
            return
        if self.guar[eng].get(sname, 0) >= val:
            return
        self.guar[eng][sname] = val
        self.q[eng].append(("w", sname, val))

    def emit(self, eng, fn, reads=(), writes=(), inc=None, k=1):
        need = {}
        for r in reads:
            for s, v in r.writers.items():
                if need.get(s, 0) < v:
                    need[s] = v
        for r in writes:
            for d in (r.writers, r.readers):
                for s, v in d.items():
                    if need.get(s, 0) < v:
                        need[s] = v
        if inc is not None and inc[0] == "d":
            need.pop(inc, None)
        for s, v in need.items():
            self._wait(eng, s, v)
        if inc is None:
            self.q[eng].append(("op", fn, None, 0))
            self.pend[eng].append((tuple(reads), tuple(writes)))
            return None
        self.cnt[inc] += k
        gate = (inc, self.cnt[inc])
        if eng == "tensor" and inc == "T":
            self.guar[eng][inc] = self.cnt[inc]
        self.q[eng].append(("op", fn, inc, k))
        for rr, ww in self.pend[eng]:
            for r in rr:
                r.read(gate)
            for w in ww:
                w.wrote(gate)
        self.pend[eng].clear()
        for r in reads:
            r.read(gate)
        for r in writes:
            r.wrote(gate)
        return gate

    def raw_wait(self, eng, gatedict):
        for s, v in gatedict.items():
            self._wait(eng, s, v)


def build_nc():
    nc = bass.Bass("TRN2", target_bir_lowering=False, debug=False)
    P = Prog()
    for s in ("T", "V", "S", "G"):
        P.sem(s)
    DMA_SEMS = ("dxy", "dsl0", "dsl1", "dpf0", "dpf1", "dgw1", "doutw",
                "dw2a", "dw2b", "dl2a", "dl2b", "dmisc", "dbias",
                "dout0", "dout1", "dsb", "dsw2", "dsw2b", "dsl2s", "dsl2t")
    for s in DMA_SEMS:
        P.sem(s)

    dt_in = lambda n, sh, dt=F32: nc.dram_tensor(n, sh, dt, kind="ExternalInput").ap()
    x_tm = dt_in("x_tm", [S, D])
    x_fm = dt_in("x_fm", [D, S])
    pos_tm = dt_in("pos_tm", [S, D])
    pos_fm = dt_in("pos_fm", [D, S])
    gmem = dt_in("gmem", [64, D])
    convw = dt_in("convw", [4097, D])
    gate_w = dt_in("gate_w", [2048, D])
    out_w = dt_in("out_w", [D, D])
    w1f = dt_in("w1f", [D, 2048])
    w2f = dt_in("w2f", [2048, FF])
    lin2_w = dt_in("lin2_w", [FF, D])
    gate_b = dt_in("gate_b", [128, 8])
    out_b = dt_in("out_b", [128, 8])
    lin2_b = dt_in("lin2_b", [128, 8])
    router_w = dt_in("router_w", [128, 64])
    router_b = dt_in("router_b", [8, 1])
    out_fm = nc.dram_tensor("out_fm", [D, S], F32, kind="ExternalOutput").ap()
    dt_sc = lambda n, sh: nc.dram_tensor(n, sh, BF16, kind="Internal").ap()
    gw_bf = dt_sc("gw_bf", [2048, D])
    ow_bf = dt_sc("ow_bf", [D, D])
    w1_bf = dt_sc("w1_bf", [D, 2048])
    w2_bf = dt_sc("w2_bf", [2048, FF])
    l2_bf = dt_sc("l2_bf", [FF, D])

    import contextlib
    stack = contextlib.ExitStack()
    sb = lambda n, sh, dt: stack.enter_context(nc.sbuf_tensor(n, sh, dt)).ap()
    ps = lambda n, sh, dt: stack.enter_context(nc.psum_tensor(n, sh, dt)).ap()

    xy = sb("xy", [128, 8192], BF16)
    ynbf = sb("ynbf", [128, 8192], BF16)
    xpef = sb("xpef", [128, 4096], BF16)
    xpet = sb("xpet", [128, 4096], BF16)
    ah = sb("ah", [128, 16384], BF16)
    tbuf = sb("tbuf", [128, 8192], BF16)
    gw1a = sb("gw1a", [128, 16384], BF16)
    outwa = sb("outwa", [128, 8192], BF16)
    w2a = sb("w2a", [128, 16384], BF16)
    gt_tm = sb("gt_tm", [128, 1024], BF16)

    gt_fm = sb("gt_fm", [128, 1024], BF16)
    ebf = sb("ebf", [128, 512], F32)
    cfm = sb("cfm", [128, 512], BF16)
    ctm = sb("ctm", [128, 64], F32)
    csb = sb("csb", [128, 512], BF16)
    mwsb = sb("mwsb", [128, 512], BF16)
    l2a = sb("l2a", [128, 4096], BF16)
    ztm = sb("ztm", [128, 32], F32)
    rztm = sb("rztm", [128, 32], BF16)
    scr4 = sb("scr4", [128, 16], F32)
    scrf = sb("scrf", [128, 640], F32)
    outst = sb("outst", [128, 512], F32)
    statv = sb("statv", [1, 1536], F32)
    ones_b = sb("ones_b", [128, 128], BF16)
    ones_f = sb("ones_f", [128, 128], F32)
    id_f = sb("id_f", [128, 128], F32)
    id_b = sb("id_b", [128, 128], BF16)
    basis = sb("basis", [8, 1024], BF16)
    neg8 = sb("neg8", [128, 1], F32)
    epsc = sb("epsc", [128, 1], F32)
    gb_s = sb("gb_s", [128, 8], F32)
    ob_s = sb("ob_s", [128, 8], F32)
    lb_s = sb("lb_s", [128, 8], F32)
    rb_s = sb("rb_s", [8, 1], F32)
    rw_s = sb("rw_s", [128, 64], BF16)
    iot = sb("iot", [128, 1], F32)

    cbrow = l2a[0:1, 0:1024]
    pb = [ps(f"pb{i}", [128, 512], F32) for i in range(8)]
    pbb = [p.bitcast(BF16) for p in pb]

    LOC, GLO, MIX, PL0, PG0, RZ0 = 0, 4096, 8192, 12288, 14336, 15360

    R = lambda: Reg()
    Rxpef, Rxpet = R(), R()
    Rxyc = [R() for _ in range(NCH)]
    Rynbfc = [[R() for _ in range(NCH)] for _ in range(2)]
    Rloc, Rglo = R(), R()
    Rmixc = [R() for _ in range(NCH)]
    Rpl = [R() for _ in range(4)]
    Rpg = [R() for _ in range(2)]
    Rrz = [R() for _ in range(2)]
    Rah = R()
    Rtb = R()
    Rslab = [R() for _ in range(6)]
    Rgw1, Routw = R(), R()
    Rw2 = [R() for _ in range(2)]
    Rconv = [R() for _ in range(4)]
    Rl2 = [R() for _ in range(4)]
    Rout = [R() for _ in range(2)]
    Rgt_tm, Rgt_fm = R(), R()
    Rgb, Rob, Rlb, Rrb, Rrw, Rcbrow = R(), R(), R(), R(), R(), R()
    Rsgw, Rsow, Rsw1, Rsw2, Rsl2 = R(), R(), R(), R(), R()
    Rebf, Rcfm, Rscr4, Rscrf, Rstat = R(), R(), R(), R(), R()
    Rctm2 = [R(), R()]
    Rzt = [R(), R()]
    Rrzt = [R(), R()]
    Rcsb = R()
    Rmwsb = [R(), R()]
    Rb = [R() for _ in range(8)]
    Rconst = R()
    Rbias = R()


    def gp(fn, **kw):
        return P.emit("gpsimd", fn, **kw)

    def vx(fn, **kw):
        return P.emit("vector", fn, **kw)

    def sx(fn, **kw):
        return P.emit("scalar", fn, **kw)

    def tx(fn, **kw):
        return P.emit("tensor", fn, **kw)

    def dmaw(sem, fn, **kw):
        return P.emit("gpsimd", fn, inc=sem, k=16, **kw)

    def dmax(sem, fn, **kw):
        return P.emit("sync", fn, inc=sem, k=16, **kw)

    gp(lambda E: E.memset(ones_b[:], 1.0), writes=[Rconst], inc="G")
    gp(lambda E: E.memset(ones_f[:], 1.0), writes=[Rconst], inc="G")
    gp(lambda E: E.memset(neg8[:], CEXP), writes=[Rconst], inc="G")
    gp(lambda E: E.memset(epsc[:], 1e-5), writes=[Rconst], inc="G")
    gp(lambda E: E.iota(id_f[:], pattern=[[1, 128]], channel_multiplier=0,
                        allow_small_or_imprecise_dtypes=True),
       writes=[Rconst], inc="G")
    gp(lambda E: E.iota(iot[:], pattern=[[0, 1]], channel_multiplier=1,
                        allow_small_or_imprecise_dtypes=True),
       writes=[Rconst], inc="G")
    vx(lambda E: E.tensor_scalar(id_f[:], id_f[:], iot[:], None, ALU.is_equal),
       reads=[Rconst], writes=[Rconst], inc="V")
    vx(lambda E: E.tensor_copy(id_b[:], id_f[:]), reads=[Rconst], writes=[Rconst], inc="V")
    gp(lambda E: E.iota(basis[:], pattern=[[1, 8], [0, 128]], channel_multiplier=0,
                        allow_small_or_imprecise_dtypes=True),
       writes=[Rconst], inc="G")
    vx(lambda E: E.tensor_scalar(basis[:], basis[:], iot[0:8, :], None, ALU.is_equal),
       reads=[Rconst], writes=[Rconst], inc="V")
    dmax("dbias", lambda E: E.dma_start(out=gb_s[:], in_=gate_b), writes=[Rgb])
    dmax("dbias", lambda E: E.dma_start(out=ob_s[:], in_=out_b), writes=[Rob])
    dmax("dbias", lambda E: E.dma_start(out=lb_s[:], in_=lin2_b), writes=[Rlb])
    dmax("dbias", lambda E: E.dma_start(out=rb_s[:], in_=router_b), writes=[Rrb])
    dmaw("dmisc", lambda E: E.dma_start(out=rw_s[:], in_=router_w), writes=[Rrw])
    dmaw("dmisc", lambda E: E.dma_start(out=gt_tm[64:128, :], in_=gmem), writes=[Rgt_tm])
    dmaw("dmisc", lambda E: E.dma_start(out=cbrow[:], in_=convw[4096:4097, :]), writes=[Rcbrow])
    conv_src = convw[0:4096, :].rearrange("(j p) c -> p j c", p=128)

    def conv_group_load(g):
        slot = g % 2
        dst = w2a[:, slot * 4096:(slot + 1) * 4096].rearrange("p (j c) -> p j c", c=1024)
        dmaw("dw2a" if slot == 0 else "dw2b",
             lambda E, g=g, d=dst: E.dma_start(out=d, in_=conv_src[:, 4 * g:4 * g + 4, :]),
             writes=[Rconv[slot]])

    for r in (Rgb, Rob, Rlb, Rrb):
        r.writers = {"dbias": P.cnt["dbias"]}
    for r in (Rrw, Rgt_tm, Rcbrow):
        r.writers = {"dmisc": P.cnt["dmisc"]}

    xfm_g0 = x_fm.rearrange("(f p) s -> p f s", p=128)
    pfm_g0 = pos_fm.rearrange("(f p) s -> p f s", p=128)
    xtm_g0 = x_tm.rearrange("(g p) d -> p g d", p=128)
    ptm_g0 = pos_tm.rearrange("(g p) d -> p g d", p=128)
    dmaw("dpf0", lambda E: E.dma_start(
        out=xpef.rearrange("p (f s) -> p f s", s=512), in_=xfm_g0[:, :, 0:512]),
         writes=[Rxpef])
    dmaw("dpf1", lambda E: E.dma_start(
        out=xpef.rearrange("p (f s) -> p f s", s=512), in_=pfm_g0[:, :, 0:512],
        accum_op=ALU.add),
         writes=[Rxpef])
    dmaw("dsl0", lambda E: E.dma_start(
        out=xpet.rearrange("p (g d) -> p g d", d=1024), in_=xtm_g0[:, 0:4, :]),
         writes=[Rxpet])
    dmaw("dsl1", lambda E: E.dma_start(
        out=xpet.rearrange("p (g d) -> p g d", d=1024), in_=ptm_g0[:, 0:4, :],
        accum_op=ALU.add),
         writes=[Rxpet])
    P.raw_wait("gpsimd", {"dpf1": P.cnt["dpf1"], "dsl1": P.cnt["dsl1"]})

    conv_group_load(0)
    conv_group_load(1)
    gw_src = gate_w.rearrange("(k p) c -> p k c", p=128)
    w1_src = w1f.rearrange("(k p) c -> p k c", p=128)
    l2_src = lin2_w.rearrange("(k p) c -> p k c", p=128)
    w2_src = w2f.rearrange("(k p) (e c) -> p e k c", p=128, c=512)
    gwb_src = gw_bf.rearrange("(k p) c -> p k c", p=128)
    owb_src = ow_bf.rearrange("(k p) c -> p k c", p=128)
    w1b_src = w1_bf.rearrange("(k p) c -> p k c", p=128)
    l2b_src = l2_bf.rearrange("(k p) c -> p k c", p=128)
    w2b_src = w2_bf.rearrange("(k p) (e c) -> p e k c", p=128, c=512)

    def wb_store(sbuf_view, dst_view, slab_reg, scr_reg, sem="dsb"):
        P.emit("sync", lambda E, s=sbuf_view, d=dst_view: E.dma_start(out=d, in_=s),
               reads=[slab_reg], writes=[scr_reg], inc=sem, k=16)

    dmaw("dgw1", lambda E: E.dma_start(out=gw1a.rearrange("p (k c) -> p k c", c=1024),
                               in_=gw_src),
         writes=[Rgw1])
    wb_store(gw1a.rearrange("p (k c) -> p k c", c=1024), gwb_src, Rgw1, Rsgw)
    ow_src = out_w.rearrange("(k p) c -> p k c", p=128)
    dmaw("doutw", lambda E: E.dma_start(out=outwa.rearrange("p (k c) -> p k c", c=1024),
                                in_=ow_src),
         writes=[Routw])
    wb_store(outwa.rearrange("p (k c) -> p k c", c=1024), owb_src, Routw, Rsow)
    for r in (Rsgw, Rsow):
        r.writers = {"dsb": P.cnt["dsb"]}
    Rgw1.readers["dsb"] = P.cnt["dsb"]
    Routw.readers["dsb"] = P.cnt["dsb"]


    def ln_stage(sc, out_kind, t, bank_ids, scratch_reg):
        sc_base = sc * 512
        bsy, bsq, bb0, bb1 = bank_ids
        if out_kind == "ln1":
            srct, srcreg = xy, (lambda fc: Rxyc[fc])
        else:
            srct, srcreg = ynbf, (lambda fc: Rynbfc[sc][fc])
        for fc in range(NCH):
            xsl = srct[:, fc * 1024 + sc_base: fc * 1024 + sc_base + 512]
            pl = ah[:, PL0 + (fc % 2) * 512: PL0 + (fc % 2) * 512 + 512]
            srg = scratch_reg(fc)
            sx(lambda E, i=xsl, o=pl: E.activation(o, i, AF.Square),
               reads=[srcreg(fc)], writes=[srg], inc="S")
            tx(lambda E, r=xsl, fc=fc: E.matmul(pb[bsy][0:1, :], ones_b[:, 0:1], r,
                                                start=(fc == 0), stop=(fc == 7)),
               reads=[srcreg(fc), Rconst], writes=[Rb[bsy]], inc="T")
            tx(lambda E, r=pl, fc=fc: E.matmul(pb[bsq][0:1, :], ones_b[:, 0:1], r,
                                               start=(fc == 0), stop=(fc == 7)),
               reads=[srg, Rconst], writes=[Rb[bsq]], inc="T")
        mean = statv[0:1, 0:512]
        var = statv[0:1, 512:1024]
        meanb = statv.bitcast(BF16)[0:1, 2048:2560]
        invb = statv.bitcast(BF16)[0:1, 2560:3072]
        vx(lambda E: E.tensor_scalar(mean, pb[bsy][0:1, :], 1.0 / 1024, None, ALU.mult),
           reads=[Rb[bsy]], writes=[Rstat], inc="V")
        vx(lambda E: E.tensor_copy(meanb, mean), reads=[Rstat], writes=[Rstat],
           inc="V")
        tx(lambda E: E.matmul(pb[bb1][:], ones_b[0:1, :], meanb, start=True, stop=True),
           reads=[Rstat, Rconst], writes=[Rb[bb1]], inc="T")
        vx(lambda E: E.tensor_mul(var, mean, mean), reads=[Rstat], writes=[Rstat],
           inc="V")
        vx(lambda E: E.scalar_tensor_tensor(var, pb[bsq][0:1, :], 1.0 / 1024, var,
                                            ALU.mult, ALU.subtract),
           reads=[Rb[bsq], Rstat], writes=[Rstat], inc="V")
        sx(lambda E: E.activation(var, var, AF.Ln, bias=epsc[0:1, :]),
           reads=[Rstat, Rconst], writes=[Rstat], inc="S")
        sx(lambda E: E.activation(invb, var, AF.Exp, scale=-0.5),
           reads=[Rstat], writes=[Rstat], inc="S")
        tx(lambda E: E.matmul(pb[bb0][:], ones_b[0:1, :], invb, start=True, stop=True),
           reads=[Rstat, Rconst], writes=[Rb[bb0]], inc="T")
        for fc in range(NCH):
            xsl = srct[:, fc * 1024 + sc_base: fc * 1024 + sc_base + 512]
            q1 = scrf[:, 0:512]
            vx(lambda E, i=xsl, o=q1: E.tensor_sub(o, i, pb[bb1][:]),
               reads=[srcreg(fc), Rb[bb1]], writes=[Rscrf], inc="V")
            if out_kind == "ln1":
                osl = ynbf[:, fc * 1024 + sc_base: fc * 1024 + sc_base + 512]
                vx(lambda E, i=q1, o=osl: E.tensor_mul(o, i, pb[bb0][:]),
                   reads=[Rscrf, Rb[bb0]], writes=[Rynbfc[sc][fc]], inc="V")
            else:
                slot = 0
                osl = outst[:, 0:512]
                vx(lambda E, i=q1, o=osl: E.tensor_mul(o, i, pb[bb0][:]),
                   reads=[Rscrf, Rb[bb0]], writes=[Rout[slot]], inc="V")
                dst = out_fm[fc * 128:(fc + 1) * 128,
                             t * 1024 + sc_base: t * 1024 + sc_base + 512]
                P.emit("sync", lambda E, s=osl, d=dst: E.dma_start(out=d, in_=s),
                       reads=[Rout[slot]], inc="dout%d" % slot, k=16)

    def router_part1(sc):
        scb = sc * 512
        bank_lg, bank_et = 2, 7
        for fc in range(NCH):
            lhs = rw_s[:, fc * 8: fc * 8 + 8]
            rhs = ynbf[:, fc * 1024 + scb: fc * 1024 + scb + 512]
            tx(lambda E, l=lhs, r=rhs, fc=fc: E.matmul(
                pb[bank_lg][0:8, :], l, r, start=(fc == 0), stop=(fc == 7)),
               reads=[Rrw, Rynbfc[sc][fc]], writes=[Rb[bank_lg]],
               inc=("T" if fc == 7 else None))
        sx(lambda E: E.activation(ebf[0:8, :], pb[bank_lg][0:8, :], AF.Exp,
                                  bias=rb_s[:]),
           reads=[Rb[bank_lg], Rrb], writes=[Rebf], inc="S")
        for u in range(4):
            tin = ebf[0:8, u * 128: u * 128 + 128]
            out = pb[bank_et][:, u * 8: u * 8 + 8]
            tx(lambda E, i=tin, o=out, u=u: E.matmul(
                o, i, id_f[0:8, 0:8], start=(u == 0), stop=(u == 3),
                is_transpose=True),
               reads=[Rebf, Rconst], writes=[Rb[bank_et]],
               inc=("T" if u == 3 else None))
        et_s = scrf[:, 576:608]
        vx(lambda E: E.tensor_copy(et_s, pb[bank_et][:, 0:32]),
           reads=[Rb[bank_et]], writes=[Rscrf], inc="V")
        et = et_s.rearrange("p (a b) -> p a b", b=8)
        zt = scr4[:, 8:12]
        rzt = scr4[:, 12:16]
        m1 = scr4[:, 0:4]
        m2 = scr4[:, 4:8]
        eq = scrf[:, 512:544]
        dd = scrf[:, 544:576]
        cts = ctm[:, sc * 32: sc * 32 + 32]
        vx(lambda E: E.reduce_sum(zt, et, axis=AX.X), reads=[Rscrf],
           writes=[Rscr4], inc="V")
        vx(lambda E: E.reciprocal(rzt, zt), reads=[Rscr4], writes=[Rscr4], inc="V")
        vx(lambda E: E.reduce_max(m1, et, axis=AX.X), reads=[Rscrf],
           writes=[Rscr4], inc="V")
        m1b = m1.rearrange("p (a b) -> p a b", b=1).broadcast_to((128, 4, 8))
        vx(lambda E: E.tensor_tensor(eq.rearrange("p (a b) -> p a b", b=8), et, m1b,
                                     ALU.is_equal),
           reads=[Rscrf, Rscr4], writes=[Rscrf], inc="V")
        vx(lambda E: E.scalar_tensor_tensor(dd, eq, -1e9, et.rearrange("p a b -> p (a b)"),
                                            ALU.mult, ALU.add),
           reads=[Rscrf], writes=[Rscrf], inc="V")
        vx(lambda E: E.reduce_max(m2, dd.rearrange("p (a b) -> p a b", b=8), axis=AX.X),
           reads=[Rscrf], writes=[Rscr4], inc="V")
        m2b = m2.rearrange("p (a b) -> p a b", b=1).broadcast_to((128, 4, 8))
        vx(lambda E: E.tensor_tensor(eq.rearrange("p (a b) -> p a b", b=8), et, m2b,
                                     ALU.is_ge),
           reads=[Rscrf, Rscr4], writes=[Rscrf], inc="V")
        vx(lambda E: E.tensor_tensor(dd.rearrange("p (a b) -> p a b", b=8),
                                     eq.rearrange("p (a b) -> p a b", b=8), et,
                                     ALU.mult),
           reads=[Rscrf], writes=[Rscrf], inc="V")
        rztb = rzt.rearrange("p (a b) -> p a b", b=1).broadcast_to((128, 4, 8))
        vx(lambda E: E.tensor_tensor(cts.rearrange("p (a b) -> p a b", b=8),
                                     dd.rearrange("p (a b) -> p a b", b=8), rztb,
                                     ALU.mult),
           reads=[Rscrf, Rscr4], writes=[Rctm2[sc]], inc="V")

    def router_part2(sc):
        bank_cf = 4
        for u in range(4):
            tin = ctm[:, sc * 32 + u * 8: sc * 32 + u * 8 + 8]
            out = pb[bank_cf][0:8, u * 128: u * 128 + 128]
            tx(lambda E, i=tin, o=out, u=u: E.matmul(
                o, i, id_f[:], start=(u == 0), stop=(u == 3),
                is_transpose=True),
               reads=[Rctm2[sc], Rconst], writes=[Rb[bank_cf]],
               inc=("T" if u == 3 else None))
        vx(lambda E: E.tensor_copy(cfm[0:8, :], pb[bank_cf][0:8, 0:512]),
           reads=[Rb[bank_cf]], writes=[Rcfm], inc="V")

    for t in range(NT):
        tb0 = t * 1024
        if t > 0:
            alias_edge(Rpl + Rpg + Rrz + [Rloc, Rglo] + Rmixc, [Rah])
        xfm_g = x_fm.rearrange("(f p) s -> p f s", p=128)
        pfm_g = pos_fm.rearrange("(f p) s -> p f s", p=128)
        xtm_g = x_tm.rearrange("(g p) d -> p g d", p=128)
        ptm_g = pos_tm.rearrange("(g p) d -> p g d", p=128)
        def emit_A(tt_, sc_):
            base = tt_ * 1024 + sc_ * 512
            if tt_ >= NT:
                return
            dmaw("dpf0", lambda E, s=xfm_g[:, :, base: base + 512],
                 d=xpef.rearrange("p (f s) -> p f s", s=512): E.dma_start(
                     out=d, in_=s),
                 writes=[Rxpef])
            dmaw("dpf1", lambda E, s=pfm_g[:, :, base: base + 512],
                 d=xpef.rearrange("p (f s) -> p f s", s=512): E.dma_start(
                     out=d, in_=s, accum_op=ALU.add),
                 writes=[Rxpef])
            g0 = base // 128
            dmaw("dsl0", lambda E, s=xtm_g[:, g0:g0 + 4, :],
                 d=xpet.rearrange("p (g d) -> p g d", d=1024): E.dma_start(
                     out=d, in_=s),
                 writes=[Rxpet])
            dmaw("dsl1", lambda E, s=ptm_g[:, g0:g0 + 4, :],
                 d=xpet.rearrange("p (g d) -> p g d", d=1024): E.dma_start(
                     out=d, in_=s, accum_op=ALU.add),
                 writes=[Rxpet])

        for sc in range(2):
            scb = sc * 512
            if sc == 0:
                for fcx in range(NCH):
                    srcx = x_fm[fcx * 128:(fcx + 1) * 128, tb0:tb0 + 1024]
                    dstx = xy[:, fcx * 1024: fcx * 1024 + 1024]
                    dmaw("dxy", lambda E, s=srcx, d=dstx: E.dma_start(out=d, in_=s),
                         writes=[Rxyc[fcx]])
                for rch in Rxyc:
                    rch.writers = {"dxy": P.cnt["dxy"]}

            scnt = 0
            pcnt = 0
            for q in range(4):
                qs = q % 2
                for hp in range(2):
                    hA = q * 4 + hp * 2
                    fc = hA // 2
                    pair0 = pcnt
                    def emit_av(hh2, pbase, kc):
                        h = q * 4 + hh2
                        av_bank = 3 + hh2
                        slot = (pair0 + kc * 2 + (hh2 - hp * 2)) % 4
                        pls = ah[:, PL0 + slot * 512: PL0 + slot * 512 + 512]
                        lhs = xpet[:, kc * 1024 + 64 * h:
                                   kc * 1024 + 64 * h + 64]
                        out = pb[av_bank][pbase:pbase + 64, :]
                        tx(lambda E, l=lhs, r=pls, o=out, kc=kc: E.matmul(
                            o, l, r, start=(kc == 0), stop=(kc == 3)),
                           reads=[Rpl[slot], Rxpet], writes=[Rb[av_bank]],
                           inc=("T" if kc == 3 else None))

                    for kc in range(4):
                        for hh2, pbase in ((hp * 2, 0), (hp * 2 + 1, 64)):
                            h = q * 4 + hh2
                            b = scnt % 3
                            scnt += 1
                            lhs = xpef[pbase:pbase + 64,
                                       fc * 512 + kc * 128:
                                       fc * 512 + kc * 128 + 128]
                            rhs = xpef[pbase:pbase + 64,
                                       fc * 512: fc * 512 + 512]
                            tx(lambda E, l=lhs, r=rhs, b=b: E.matmul(
                                pb[b][:], l, r, start=True, stop=True),
                               reads=[Rxpef], writes=[Rb[b]], inc="T")
                            slot = pcnt % 4
                            pcnt += 1
                            pls = ah[:, PL0 + slot * 512: PL0 + slot * 512 + 512]
                            zc = ztm[:, qs * 16 + hh2 * 4 + kc:
                                     qs * 16 + hh2 * 4 + kc + 1]
                            sx(lambda E, b=b, o=pls, z=zc: E.activation(
                                o, pb[b][:], AF.Exp, bias=neg8[:], scale=SCALE,
                                accum_out=z),
                               reads=[Rb[b], Rconst], writes=[Rpl[slot], Rzt[qs]],
                               inc="S")
                        if kc >= 1:
                            emit_av(hp * 2, 0, kc - 1)
                            emit_av(hp * 2 + 1, 64, kc - 1)
                    emit_av(hp * 2, 0, 3)
                    emit_av(hp * 2 + 1, 64, 3)
                vx(lambda E, qs=qs: E.reciprocal(rztm[:, qs * 16: qs * 16 + 16],
                                                 ztm[:, qs * 16: qs * 16 + 16]),
                   reads=[Rzt[qs]], writes=[Rrzt[qs]], inc="V")
                for hh in range(4):
                    h = q * 4 + hh
                    fc, pbase = h // 2, 64 * (h % 2)
                    hbase = (h % 2) * 64
                    for kc in range(4):
                        rc = rztm[:, qs * 16 + hh * 4 + kc:
                                  qs * 16 + hh * 4 + kc + 1].broadcast_to((128, 64))
                        out = pb[7][hbase:hbase + 64, kc * 128:kc * 128 + 128]
                        tx(lambda E, i=rc, o=out, kc=kc: E.matmul(
                            o, i, id_b[:], start=(kc == 0), stop=(kc == 3)),
                           reads=[Rrzt[qs], Rconst], writes=[Rb[7]],
                           inc=("T" if kc == 3 else None))
                    if hh % 2 == 1:
                        vx(lambda E: E.tensor_copy(csb[:], pb[7][:]),
                           reads=[Rb[7]], writes=[Rcsb], inc="V")
                        avA, avB = 3 + hh - 1, 3 + hh
                        lA = ah[0:64, LOC + fc * 512: LOC + fc * 512 + 512]
                        lB = ah[64:128, LOC + fc * 512: LOC + fc * 512 + 512]
                        vx(lambda E, o=lA, ab=avA: E.tensor_mul(
                            o, pb[ab][0:64, :], csb[0:64, :]),
                           reads=[Rb[avA], Rcsb], writes=[Rloc], inc="V")
                        vx(lambda E, o=lB, ab=avB: E.tensor_mul(
                            o, pb[ab][64:128, :], csb[64:128, :]),
                           reads=[Rb[avB], Rcsb], writes=[Rloc], inc="V")

            if sc == 1:
                router_part1(0)

            if t == 0 and sc == 0:
                for j in range(32):
                    kc, ic = j // 8, j % 8
                    g, jj = j // 4, j % 4
                    slot = g % 2
                    if jj == 0 and g >= 2:
                        conv_group_load(g)
                    lhs = xpef[:, ic * 512: ic * 512 + 256].rearrange(
                        "p (t f) -> p t f", f=4)[:, :, kc]
                    for oh in range(2):
                        bank = 3 + oh
                        rhs = w2a[:, slot * 4096 + jj * 1024 + oh * 512:
                                  slot * 4096 + jj * 1024 + oh * 512 + 512]
                        tx(lambda E, l=lhs, r=rhs, b=bank, j=j: E.matmul(
                            pb[b][0:64, :], l, r, start=(j == 0), stop=False),
                           reads=[Rxpef, Rconv[slot]], writes=[Rb[bank]],
                           inc=("T" if (j % 4) == 3 and oh == 1 else None))
                for oh in range(2):
                    bank = 3 + oh
                    rhsb = cbrow[0:1, oh * 512: oh * 512 + 512]
                    tx(lambda E, r=rhsb, b=bank: E.matmul(
                        pb[b][0:64, :], ones_b[0:1, 0:64], r, start=False, stop=True),
                       reads=[Rcbrow, Rconst], writes=[Rb[bank]], inc="T")
                    vx(lambda E, b=bank, oh=oh: E.tensor_copy(
                        gt_tm[0:64, oh * 512: oh * 512 + 512], pb[b][0:64, :]),
                       reads=[Rb[bank]], writes=[Rgt_tm], inc="V")
                for fc in range(NCH):
                    bank = 5
                    tin = gt_tm[:, fc * 128: fc * 128 + 128]
                    tx(lambda E, i=tin, b=bank: E.matmul(
                        pbb[b][:, 0:128], i, id_b[:], start=True, stop=True,
                        is_transpose=True),
                       reads=[Rgt_tm, Rconst], writes=[Rb[bank]], inc="T")
                    vx(lambda E, fc=fc, b=bank: E.tensor_copy(
                        gt_fm[:, fc * 128: fc * 128 + 128], pbb[b][:, 0:128]),
                       reads=[Rb[bank]], writes=[Rgt_fm], inc="V")

            scnt = 0
            for q in range(4):
                qs = q % 2
                for hh in range(4):
                    h = q * 4 + hh
                    fc, pbase = h // 2, 64 * (h % 2)
                    b = scnt % 2
                    scnt += 1
                    lhs = gt_fm[pbase:pbase + 64, fc * 128: fc * 128 + 128]
                    rhs = xpef[pbase:pbase + 64, fc * 512: fc * 512 + 512]
                    tx(lambda E, l=lhs, r=rhs, b=b: E.matmul(
                        pb[b][:], l, r, start=True, stop=True),
                       reads=[Rgt_fm, Rxpef], writes=[Rb[b]], inc="T")
                    slot = h % 2
                    pgs = ah[:, PG0 + slot * 512: PG0 + slot * 512 + 512]
                    sx(lambda E, b=b, o=pgs: E.activation(o, pb[b][:], AF.Exp,
                                                          bias=neg8[:], scale=SCALE),
                       reads=[Rb[b], Rconst], writes=[Rpg[slot]], inc="S")
                    for tt in range(4):
                        zout = pb[2][:, qs * 16 + hh * 4 + tt: qs * 16 + hh * 4 + tt + 1]
                        tx(lambda E, l=pgs, o=zout, tt=tt, hh=hh, qs=qs: E.matmul(
                            o, l[:, tt * 128:tt * 128 + 128], ones_b[:, 0:1],
                            start=(hh == 0 and tt == 0), stop=(hh == 3 and tt == 3)),
                           reads=[Rpg[slot], Rconst], writes=[Rb[2]],
                           inc=("T" if (hh == 3 and tt == 3) else None))
                    av_bank = 3 + hh
                    lhs2 = gt_tm[:, h * 64: h * 64 + 64]
                    out = pb[av_bank][pbase:pbase + 64, :]
                    tx(lambda E, l=lhs2, r=pgs, o=out: E.matmul(
                        o, l, r, start=True, stop=True),
                       reads=[Rpg[slot], Rgt_tm], writes=[Rb[av_bank]], inc="T")
                vx(lambda E, qs=qs: E.reciprocal(rztm[:, qs * 16: qs * 16 + 16],
                                                 pb[2][:, qs * 16: qs * 16 + 16]),
                   reads=[Rb[2]], writes=[Rrzt[qs]], inc="V")
                for hh in range(4):
                    h = q * 4 + hh
                    fc, pbase = h // 2, 64 * (h % 2)
                    hbase = (h % 2) * 64
                    for kc in range(4):
                        rc = rztm[:, qs * 16 + hh * 4 + kc:
                                  qs * 16 + hh * 4 + kc + 1].broadcast_to((128, 64))
                        out = pb[7][hbase:hbase + 64, kc * 128:kc * 128 + 128]
                        tx(lambda E, i=rc, o=out, kc=kc: E.matmul(
                            o, i, id_b[:], start=(kc == 0), stop=(kc == 3)),
                           reads=[Rrzt[qs], Rconst], writes=[Rb[7]],
                           inc=("T" if kc == 3 else None))
                    if hh % 2 == 1:
                        vx(lambda E: E.tensor_copy(csb[:], pb[7][:]),
                           reads=[Rb[7]], writes=[Rcsb], inc="V")
                        avA, avB = 3 + hh - 1, 3 + hh
                        gA = ah[0:64, GLO + fc * 512: GLO + fc * 512 + 512]
                        gB = ah[64:128, GLO + fc * 512: GLO + fc * 512 + 512]
                        vx(lambda E, o=gA, ab=avA: E.tensor_mul(
                            o, pb[ab][0:64, :], csb[0:64, :]),
                           reads=[Rb[avA], Rcsb], writes=[Rglo], inc="V")
                        vx(lambda E, o=gB, ab=avB: E.tensor_mul(
                            o, pb[ab][64:128, :], csb[64:128, :]),
                           reads=[Rb[avB], Rcsb], writes=[Rglo], inc="V")

            emit_A(t + (sc == 1), (sc + 1) % 2)

            for m in range(NCH):
                bank = m % 3
                for k in range(16):
                    lhs = gw1a[:, k * 1024 + m * 128: k * 1024 + m * 128 + 128]
                    base = LOC if k < 8 else GLO
                    rr = Rloc if k < 8 else Rglo
                    rhs = ah[:, base + (k % 8) * 512: base + (k % 8) * 512 + 512]
                    tx(lambda E, l=lhs, r=rhs, b=bank, k=k: E.matmul(
                        pb[b][:], l, r, start=(k == 0), stop=(k == 15)),
                       reads=[Rgw1, rr], writes=[Rb[bank]],
                       inc=("T" if k == 15 else None))
                msl = ah[:, MIX + m * 512: MIX + m * 512 + 512]
                sx(lambda E, b=bank, o=msl, m=m: E.activation(
                    o, pb[b][:], AF.Sigmoid, bias=gb_s[:, m:m + 1]),
                   reads=[Rb[bank], Rgb], writes=[Rmixc[m]], inc="S")
                lsl = ah[:, LOC + m * 512: LOC + m * 512 + 512]
                gsl = ah[:, GLO + m * 512: GLO + m * 512 + 512]
                t1 = ah[:, PL0 + (m % 4) * 512: PL0 + (m % 4) * 512 + 512]
                vx(lambda E, a=lsl, b=gsl, o=t1: E.tensor_sub(o, a, b),
                   reads=[Rloc, Rglo], writes=[Rpl[m % 4]], inc="V")
                vx(lambda E, a=t1, b=msl, o=t1: E.tensor_mul(o, a, b),
                   reads=[Rpl[m % 4], Rmixc[m]], writes=[Rpl[m % 4]], inc="V")
                vx(lambda E, a=t1, b=gsl, o=msl: E.tensor_add(o, a, b),
                   reads=[Rpl[m % 4], Rglo], writes=[Rmixc[m]], inc="V")
            for m in range(NCH):
                bank = 3 + (m % 3)
                for k in range(NCH):
                    lhs = outwa[:, k * 1024 + m * 128: k * 1024 + m * 128 + 128]
                    rhs = ah[:, MIX + k * 512: MIX + k * 512 + 512]
                    tx(lambda E, l=lhs, r=rhs, b=bank, k=k: E.matmul(
                        pb[b][:], l, r, start=(k == 0), stop=(k == 7)),
                       reads=[Routw, Rmixc[k]], writes=[Rb[bank]],
                       inc=("T" if k == 7 else None))
                xsl = xy[:, m * 1024 + scb: m * 1024 + scb + 512]
                vx(lambda E, b=bank, o=xsl, m=m: E.scalar_tensor_tensor(
                    o, pb[b][:], ob_s[:, m:m + 1], o, ALU.add, ALU.add),
                   reads=[Rb[bank], Rob, Rxyc[m]], writes=[Rxyc[m]], inc="V")
            ln_stage(sc, "ln1", t, (0, 1, 6, 7), lambda fc: Rpl[fc % 2])

        alias_edge([Rah], Rpl + Rpg + Rrz + [Rloc, Rglo] + Rmixc)
        if t == 0:
            dmaw("dgw1", lambda E: E.dma_start(
                out=gw1a.rearrange("p (k c) -> p k c", c=2048), in_=w1_src),
                 writes=[Rgw1])
            wb_store(gw1a.rearrange("p (k c) -> p k c", c=2048), w1b_src, Rgw1, Rsw1)
        else:
            P.emit("gpsimd", lambda E: E.dma_start(
                out=gw1a.rearrange("p (k c) -> p k c", c=2048), in_=w1b_src),
                reads=[Rsw1], writes=[Rgw1], inc="dgw1", k=16)
        if t == 0:
            alias_edge(Rw2, Rconv)

        for sc in range(2):
            scb = sc * 512
            router_part2(sc)

            for er in range(16):
                e = er // 2
                mslot = 0
                msb = mwsb[:, 0:512]
                if er % 2 == 0:
                    mwb_bank = 5 + mslot
                    lhs = basis[0:8, e * 128: e * 128 + 128]
                    tx(lambda E, l=lhs, b=mwb_bank: E.matmul(pb[b][:], l, cfm[0:8, :],
                                                             start=True, stop=True),
                       reads=[Rcfm, Rconst], writes=[Rb[mwb_bank]], inc="T")
                    sx(lambda E, b=mwb_bank, o=msb: E.copy(o, pb[b][:]),
                       reads=[Rb[mwb_bank]], writes=[Rmwsb[mslot]], inc="S")
                bank = er % 3
                for k in range(NCH):
                    lhs = gw1a[:, k * 2048 + er * 128: k * 2048 + er * 128 + 128]
                    rhs = ynbf[:, k * 1024 + scb: k * 1024 + scb + 512]
                    tx(lambda E, l=lhs, r=rhs, b=bank, k=k: E.matmul(
                        pb[b][:], l, r, start=(k == 0), stop=(k == 7)),
                       reads=[Rgw1, Rynbfc[sc][k]], writes=[Rb[bank]],
                       inc=("T" if k == 7 else None))
                tsl = tbuf[:, er * 512: er * 512 + 512]
                vx(lambda E, b=bank, o=tsl, m=msb: E.tensor_mul(o, pb[b][:], m),
                   reads=[Rb[bank], Rmwsb[mslot]], writes=[Rtb], inc="V")

            for e8 in range(8):
                reg = e8 % 2
                dst = w2a[:, reg * 8192: reg * 8192 + 8192].rearrange(
                    "p (k c) -> p k c", c=512)
                if t == 0 and sc == 0:
                    dmaw("dw2a" if reg == 0 else "dw2b",
                         lambda E, d=dst, e8=e8: E.dma_start(out=d, in_=w2_src[:, e8]),
                         writes=[Rw2[reg]])
                    wb_store(dst, w2b_src[:, e8], Rw2[reg], Rsw2,
                             "dsw2" if reg == 0 else "dsw2b")
                else:
                    P.emit("gpsimd",
                           lambda E, d=dst, e8=e8: E.dma_start(out=d, in_=w2b_src[:, e8]),
                           reads=[Rsw2], writes=[Rw2[reg]],
                           inc=("dw2a" if reg == 0 else "dw2b"), k=16)
                for f4 in range(4):
                    f = e8 * 4 + f4
                    bank = f % 3
                    for k in range(16):
                        lhs = w2a[:, reg * 8192 + k * 512 + f4 * 128:
                                  reg * 8192 + k * 512 + f4 * 128 + 128]
                        rhs = tbuf[:, k * 512: k * 512 + 512]
                        tx(lambda E, l=lhs, r=rhs, b=bank, k=k: E.matmul(
                            pb[b][:], l, r, start=(k == 0), stop=(k == 15)),
                           reads=[Rw2[reg], Rtb], writes=[Rb[bank]],
                           inc=("T" if k == 15 else None))
                    hsl = ah[:, f * 512: f * 512 + 512]
                    sx(lambda E, b=bank, o=hsl: E.activation(o, pb[b][:], AF.Gelu),
                       reads=[Rb[bank]], writes=[Rah], inc="S")

            if t == 0 and sc == 0:
                Rsw2.writers = {"dsw2": P.cnt["dsw2"], "dsw2b": P.cnt["dsw2b"]}
            if sc == 0:
                router_part1(1)
            if t == 0 and sc == 0:
                alias_edge([Rl2[0]], [Rcbrow])
            for kf in range(32):
                slot = (kf // 2) % 2
                if kf % 2 == 0:
                    dst = l2a[:, slot * 2048: slot * 2048 + 2048].rearrange(
                        "p (k c) -> p k c", c=1024)
                    if t == 0 and sc == 0:
                        dmaw("dl2a" if slot == 0 else "dl2b",
                             lambda E, d=dst, kf=kf: E.dma_start(
                                 out=d, in_=l2_src[:, kf:kf + 2, :]),
                             writes=[Rl2[slot]])
                        wb_store(dst, l2b_src[:, kf:kf + 2, :], Rl2[slot], Rsl2,
                                 "dsl2s" if slot == 0 else "dsl2t")
                    else:
                        P.emit("sync",
                               lambda E, d=dst, kf=kf: E.dma_start(
                                   out=d, in_=l2b_src[:, kf:kf + 2, :]),
                               reads=[Rsl2], writes=[Rl2[slot]],
                               inc=("dl2a" if slot == 0 else "dl2b"), k=16)
                off = (kf % 2) * 1024
                for m in range(NCH):
                    lhs = l2a[:, slot * 2048 + off + m * 128:
                              slot * 2048 + off + m * 128 + 128]
                    rhs = ah[:, kf * 512: kf * 512 + 512]
                    tx(lambda E, l=lhs, r=rhs, b=m, kf=kf: E.matmul(
                        pb[b][:], l, r, start=(kf == 0), stop=(kf == 31)),
                       reads=[Rl2[slot], Rah], writes=[Rb[m]],
                       inc=("T" if (kf % 2 == 1 and m == NCH - 1) else None))
            if t == 0 and sc == 0:
                Rsl2.writers = {"dsl2s": P.cnt["dsl2s"], "dsl2t": P.cnt["dsl2t"]}
            for m in range(NCH):
                ysl = ynbf[:, m * 1024 + scb: m * 1024 + scb + 512]
                vx(lambda E, b=m, y=ysl, m=m: E.scalar_tensor_tensor(
                    y, pb[b][:], lb_s[:, m:m + 1], y, ALU.add, ALU.add),
                   reads=[Rb[m], Rlb, Rynbfc[sc][m]], writes=[Rynbfc[sc][m]],
                   inc="V")
            ln_stage(sc, "ln2", t, (0, 1, 6, 7), lambda fc: Rah)

        if t + 1 < NT:
            P.emit("gpsimd", lambda E: E.dma_start(
                out=gw1a.rearrange("p (k c) -> p k c", c=1024), in_=gwb_src),
                reads=[Rsgw], writes=[Rgw1], inc="dgw1", k=16)
            P.emit("gpsimd", lambda E: E.dma_start(
                out=outwa.rearrange("p (k c) -> p k c", c=1024), in_=owb_src),
                reads=[Rsow], writes=[Routw], inc="doutw", k=16)

    P.raw_wait("sync", {"dout0": P.cnt["dout0"], "dout1": P.cnt["dout1"]})

    semh = {}
    for name in ("T", "V", "S", "G") + DMA_SEMS:
        semh[name] = stack.enter_context(nc.semaphore(name))

    ctx_lp = nc.allow_low_precision(
        reason="bf16 operands for matmuls are intended; fp32 PSUM accumulation")
    ctx_lp.__enter__()
    with nc.Block() as block:
        def run_queue(qname):
            def runner(E):
                for item in P.q[qname]:
                    if item[0] == "w":
                        E.wait_ge(semh[item[1]], item[2])
                    else:
                        _, fn, inc, k = item
                        ins = fn(E)
                        if inc is not None:
                            ins.then_inc(semh[inc], k)
            return runner

        block.tensor(run_queue("tensor"))
        block.vector(run_queue("vector"))
        block.scalar(run_queue("scalar"))
        block.sync(run_queue("sync"))
        block.gpsimd(run_queue("gpsimd"))

    ctx_lp.__exit__(None, None, None)
    stack.close()
    return nc


_NC_CACHE = None


def kernel(L0_attn_x, L0_attn_pos_emb, L0_attn_global_memory, L0_attn_conv_w,
           L0_attn_conv_b, L0_attn_gate_w, L0_attn_gate_b, L0_attn_out_w,
           L0_attn_out_b, L0_ffn_router_w, L0_ffn_router_b, L0_ffn_w1,
           L0_ffn_w2, L0_ffn_lin2_w, L0_ffn_lin2_b):
    global _NC_CACHE
    if _NC_CACHE is None:
        _NC_CACHE = build_nc()
    nc = _NC_CACHE

    f32 = np.float32
    ac = np.ascontiguousarray
    pos = ac(L0_attn_pos_emb[:S], dtype=f32)
    pos_f = ac(pos.T)
    convw = np.concatenate(
        [L0_attn_conv_w.transpose(2, 1, 0).reshape(4096, D),
         L0_attn_conv_b.reshape(1, D)], axis=0)
    convw = ac(convw, dtype=f32)
    gate_w = ac(L0_attn_gate_w, dtype=f32)
    out_w = ac(L0_attn_out_w, dtype=f32)
    w1fh = ac(L0_ffn_w1.transpose(1, 0, 2).reshape(D, 2048), dtype=f32)
    w2fh = ac(L0_ffn_w2.reshape(2048, FF), dtype=f32)
    lin2 = ac(L0_ffn_lin2_w, dtype=f32)
    gb = ac(L0_attn_gate_b.reshape(8, 128).T, dtype=f32)
    ob = ac(L0_attn_out_b.reshape(8, 128).T, dtype=f32)
    lb = ac(L0_ffn_lin2_b.reshape(8, 128).T, dtype=f32)
    rw = ac(L0_ffn_router_w.reshape(8, 128, 8).transpose(1, 0, 2).reshape(128, 64),
            dtype=f32)
    rb = ac(L0_ffn_router_b.reshape(8, 1), dtype=f32)
    gm = ac(L0_attn_global_memory[0], dtype=f32)

    in_maps = []
    for b in range(8):
        xb = ac(L0_attn_x[b], dtype=f32)
        in_maps.append({
            "x_tm": xb, "x_fm": ac(xb.T),
            "pos_tm": pos, "pos_fm": pos_f,
            "gmem": gm, "convw": convw,
            "gate_w": gate_w, "out_w": out_w,
            "w1f": w1fh, "w2f": w2fh, "lin2_w": lin2,
            "gate_b": gb, "out_b": ob, "lin2_b": lb,
            "router_w": rw, "router_b": rb,
        })
    res = run_bass_kernel_spmd(nc, in_maps, core_ids=list(range(8)))
    out = np.empty((8, S, D), np.float32)
    for b in range(8):
        out[b] = res.results[b]["out_fm"].T
    return out


# version 17
# speedup vs baseline: 1.1924x; 1.0359x over previous
import numpy as np
import concourse.bass as bass
import concourse.mybir as mybir
from concourse.bass_utils import run_bass_kernel_spmd

F32 = mybir.dt.float32
BF16 = mybir.dt.bfloat16
AF = mybir.ActivationFunctionType
ALU = mybir.AluOpType
AX = mybir.AxisListType

S, D, H, HD, WIN = 4096, 1024, 16, 64, 512
TILE, NT, SC = 1024, 4, 512
NCH = 8
E8, RR, FF = 8, 256, 4096
SCALE = 0.125
CEXP = -8.0

ENGINES = ("tensor", "vector", "scalar", "sync", "gpsimd")


class Reg:
    __slots__ = ("writers", "readers")

    def __init__(self):
        self.writers = {}
        self.readers = {}

    def wrote(self, gate):
        s, v = gate
        self.writers = {s: v}
        self.readers = {}

    def read(self, gate):
        s, v = gate
        if self.readers.get(s, 0) < v:
            self.readers[s] = v

def alias_edge(dsts, srcs):
    merged = {}
    for o in srcs:
        for d in (o.writers, o.readers):
            for s, v in d.items():
                if merged.get(s, 0) < v:
                    merged[s] = v
    for o in srcs:
        o.writers = {}
        o.readers = {}
    for r in dsts:
        r.writers = {}
        r.readers = dict(merged)


class Prog:
    def __init__(self):
        self.q = {e: [] for e in ENGINES}
        self.cnt = {}
        self.guar = {e: {} for e in ENGINES}
        self.pend = {e: [] for e in ENGINES}

    def sem(self, name):
        self.cnt[name] = 0
        return name

    def _wait(self, eng, sname, val):
        if val <= 0:
            return
        if self.guar[eng].get(sname, 0) >= val:
            return
        self.guar[eng][sname] = val
        self.q[eng].append(("w", sname, val))

    def emit(self, eng, fn, reads=(), writes=(), inc=None, k=1):
        need = {}
        for r in reads:
            for s, v in r.writers.items():
                if need.get(s, 0) < v:
                    need[s] = v
        for r in writes:
            for d in (r.writers, r.readers):
                for s, v in d.items():
                    if need.get(s, 0) < v:
                        need[s] = v
        if inc is not None and inc[0] == "d":
            need.pop(inc, None)
        for s, v in need.items():
            self._wait(eng, s, v)
        if inc is None:
            self.q[eng].append(("op", fn, None, 0))
            self.pend[eng].append((tuple(reads), tuple(writes)))
            return None
        self.cnt[inc] += k
        gate = (inc, self.cnt[inc])
        if eng == "tensor" and inc == "T":
            self.guar[eng][inc] = self.cnt[inc]
        self.q[eng].append(("op", fn, inc, k))
        for rr, ww in self.pend[eng]:
            for r in rr:
                r.read(gate)
            for w in ww:
                w.wrote(gate)
        self.pend[eng].clear()
        for r in reads:
            r.read(gate)
        for r in writes:
            r.wrote(gate)
        return gate

    def raw_wait(self, eng, gatedict):
        for s, v in gatedict.items():
            self._wait(eng, s, v)


def build_nc():
    nc = bass.Bass("TRN2", target_bir_lowering=False, debug=False)
    P = Prog()
    for s in ("T", "V", "S", "G"):
        P.sem(s)
    DMA_SEMS = ("dxy", "dsl0", "dsl1", "dpf0", "dpf1", "dgw1", "doutw",
                "dw2a", "dw2b", "dl2a", "dl2b", "dl2c", "dl2d",
                "dmisc", "dbias",
                "dout0", "dout1", "dsb", "dsw2", "dsw2b", "dsl2s", "dsl2t")
    L2SEM = ("dl2a", "dl2b", "dl2c", "dl2d")
    for s in DMA_SEMS:
        P.sem(s)

    dt_in = lambda n, sh, dt=F32: nc.dram_tensor(n, sh, dt, kind="ExternalInput").ap()
    x_tm = dt_in("x_tm", [S, D])
    x_fm = dt_in("x_fm", [D, S])
    pos_tm = dt_in("pos_tm", [S, D])
    pos_fm = dt_in("pos_fm", [D, S])
    gmem = dt_in("gmem", [64, D])
    convw = dt_in("convw", [4097, D])
    gate_w = dt_in("gate_w", [2048, D])
    out_w = dt_in("out_w", [D, D])
    w1f = dt_in("w1f", [D, 2048])
    w2f = dt_in("w2f", [2048, FF])
    lin2_w = dt_in("lin2_w", [FF, D])
    gate_b = dt_in("gate_b", [128, 8])
    out_b = dt_in("out_b", [128, 8])
    lin2_b = dt_in("lin2_b", [128, 8])
    router_w = dt_in("router_w", [128, 64])
    router_b = dt_in("router_b", [8, 1])
    out_fm = nc.dram_tensor("out_fm", [D, S], F32, kind="ExternalOutput").ap()
    dt_sc = lambda n, sh: nc.dram_tensor(n, sh, BF16, kind="Internal").ap()
    gw_bf = dt_sc("gw_bf", [2048, D])
    ow_bf = dt_sc("ow_bf", [D, D])
    w1_bf = dt_sc("w1_bf", [D, 2048])
    w2_bf = dt_sc("w2_bf", [2048, FF])
    l2_bf = dt_sc("l2_bf", [FF, D])

    import contextlib
    stack = contextlib.ExitStack()
    sb = lambda n, sh, dt: stack.enter_context(nc.sbuf_tensor(n, sh, dt)).ap()
    ps = lambda n, sh, dt: stack.enter_context(nc.psum_tensor(n, sh, dt)).ap()

    xy = sb("xy", [128, 8192], BF16)
    ynbf = sb("ynbf", [128, 8192], BF16)
    xpef = sb("xpef", [128, 4096], BF16)
    xpet = sb("xpet", [128, 4096], BF16)
    ah = sb("ah", [128, 16384], BF16)
    tbuf = sb("tbuf", [128, 8192], BF16)
    gw1a = sb("gw1a", [128, 16384], BF16)
    outwa = sb("outwa", [128, 8192], BF16)
    w2a = sb("w2a", [128, 16384], BF16)
    gt_tm = sb("gt_tm", [128, 1024], BF16)

    gt_fm = sb("gt_fm", [128, 1024], BF16)
    ebf = sb("ebf", [128, 512], F32)
    cfm = sb("cfm", [128, 512], BF16)
    ctm = sb("ctm", [128, 64], F32)
    csb = sb("csb", [128, 512], BF16)
    mwsb = sb("mwsb", [128, 512], BF16)
    l2a = sb("l2a", [128, 4096], BF16)
    ztm = sb("ztm", [128, 32], F32)
    rztm = sb("rztm", [128, 32], BF16)
    scr4 = sb("scr4", [128, 16], F32)
    scrf = sb("scrf", [128, 640], F32)
    outst = sb("outst", [128, 512], F32)
    statv = sb("statv", [1, 1536], F32)
    ones_b = sb("ones_b", [128, 128], BF16)
    ones_f = sb("ones_f", [128, 128], F32)
    id_f = sb("id_f", [128, 128], F32)
    id_b = sb("id_b", [128, 128], BF16)
    basis = sb("basis", [8, 1024], BF16)
    neg8 = sb("neg8", [128, 1], F32)
    epsc = sb("epsc", [128, 1], F32)
    gb_s = sb("gb_s", [128, 8], F32)
    ob_s = sb("ob_s", [128, 8], F32)
    lb_s = sb("lb_s", [128, 8], F32)
    rb_s = sb("rb_s", [8, 1], F32)
    rw_s = sb("rw_s", [128, 64], BF16)
    iot = sb("iot", [128, 1], F32)

    cbrow = l2a[0:1, 0:1024]
    pb = [ps(f"pb{i}", [128, 512], F32) for i in range(8)]
    pbb = [p.bitcast(BF16) for p in pb]

    LOC, GLO, MIX, PL0, PG0, RZ0 = 0, 4096, 8192, 12288, 14336, 15360

    R = lambda: Reg()
    Rxpef, Rxpet = R(), R()
    Rxyc = [R() for _ in range(NCH)]
    Rynbfc = [[R() for _ in range(NCH)] for _ in range(2)]
    Rloc, Rglo = R(), R()
    Rmixc = [R() for _ in range(NCH)]
    Rpl = [R() for _ in range(4)]
    Rpg = [R() for _ in range(2)]
    Rrz = [R() for _ in range(2)]
    Rah = R()
    Rtb = R()
    Rslab = [R() for _ in range(6)]
    Rgw1, Routw = R(), R()
    Rw2 = [R() for _ in range(2)]
    Rconv = [R() for _ in range(4)]
    Rl2 = [R() for _ in range(4)]
    Rout = [R() for _ in range(2)]
    Rgt_tm, Rgt_fm = R(), R()
    Rgb, Rob, Rlb, Rrb, Rrw, Rcbrow = R(), R(), R(), R(), R(), R()
    Rsgw, Rsow, Rsw1, Rsw2, Rsl2 = R(), R(), R(), R(), R()
    Rebf, Rcfm, Rscr4, Rscrf, Rstat = R(), R(), R(), R(), R()
    Rctm2 = [R(), R()]
    Rzt = [R(), R()]
    Rrzt = [R(), R()]
    Rcsb = R()
    Rmwsb = [R(), R()]
    Rb = [R() for _ in range(8)]
    Rconst = R()
    Rbias = R()


    def gp(fn, **kw):
        return P.emit("gpsimd", fn, **kw)

    def vx(fn, **kw):
        return P.emit("vector", fn, **kw)

    def sx(fn, **kw):
        return P.emit("scalar", fn, **kw)

    def tx(fn, **kw):
        return P.emit("tensor", fn, **kw)

    def dmaw(sem, fn, **kw):
        return P.emit("gpsimd", fn, inc=sem, k=16, **kw)

    def dmax(sem, fn, **kw):
        return P.emit("sync", fn, inc=sem, k=16, **kw)

    gp(lambda E: E.memset(ones_b[:], 1.0), writes=[Rconst], inc="G")
    gp(lambda E: E.memset(ones_f[:], 1.0), writes=[Rconst], inc="G")
    gp(lambda E: E.memset(neg8[:], CEXP), writes=[Rconst], inc="G")
    gp(lambda E: E.memset(epsc[:], 1e-5), writes=[Rconst], inc="G")
    gp(lambda E: E.iota(id_f[:], pattern=[[1, 128]], channel_multiplier=0,
                        allow_small_or_imprecise_dtypes=True),
       writes=[Rconst], inc="G")
    gp(lambda E: E.iota(iot[:], pattern=[[0, 1]], channel_multiplier=1,
                        allow_small_or_imprecise_dtypes=True),
       writes=[Rconst], inc="G")
    vx(lambda E: E.tensor_scalar(id_f[:], id_f[:], iot[:], None, ALU.is_equal),
       reads=[Rconst], writes=[Rconst], inc="V")
    vx(lambda E: E.tensor_copy(id_b[:], id_f[:]), reads=[Rconst], writes=[Rconst], inc="V")
    gp(lambda E: E.iota(basis[:], pattern=[[1, 8], [0, 128]], channel_multiplier=0,
                        allow_small_or_imprecise_dtypes=True),
       writes=[Rconst], inc="G")
    vx(lambda E: E.tensor_scalar(basis[:], basis[:], iot[0:8, :], None, ALU.is_equal),
       reads=[Rconst], writes=[Rconst], inc="V")
    dmax("dbias", lambda E: E.dma_start(out=gb_s[:], in_=gate_b), writes=[Rgb])
    dmax("dbias", lambda E: E.dma_start(out=ob_s[:], in_=out_b), writes=[Rob])
    dmax("dbias", lambda E: E.dma_start(out=lb_s[:], in_=lin2_b), writes=[Rlb])
    dmax("dbias", lambda E: E.dma_start(out=rb_s[:], in_=router_b), writes=[Rrb])
    dmaw("dmisc", lambda E: E.dma_start(out=rw_s[:], in_=router_w), writes=[Rrw])
    dmaw("dmisc", lambda E: E.dma_start(out=gt_tm[64:128, :], in_=gmem), writes=[Rgt_tm])
    dmaw("dmisc", lambda E: E.dma_start(out=cbrow[:], in_=convw[4096:4097, :]), writes=[Rcbrow])
    conv_src = convw[0:4096, :].rearrange("(j p) c -> p j c", p=128)

    def conv_group_load(g):
        slot = g % 2
        dst = w2a[:, slot * 4096:(slot + 1) * 4096].rearrange("p (j c) -> p j c", c=1024)
        dmaw("dw2a" if slot == 0 else "dw2b",
             lambda E, g=g, d=dst: E.dma_start(out=d, in_=conv_src[:, 4 * g:4 * g + 4, :]),
             writes=[Rconv[slot]])

    for r in (Rgb, Rob, Rlb, Rrb):
        r.writers = {"dbias": P.cnt["dbias"]}
    for r in (Rrw, Rgt_tm, Rcbrow):
        r.writers = {"dmisc": P.cnt["dmisc"]}

    xfm_g0 = x_fm.rearrange("(f p) s -> p f s", p=128)
    pfm_g0 = pos_fm.rearrange("(f p) s -> p f s", p=128)
    xtm_g0 = x_tm.rearrange("(g p) d -> p g d", p=128)
    ptm_g0 = pos_tm.rearrange("(g p) d -> p g d", p=128)
    dmaw("dpf0", lambda E: E.dma_start(
        out=xpef.rearrange("p (f s) -> p f s", s=512), in_=xfm_g0[:, :, 0:512]),
         writes=[Rxpef])
    dmaw("dpf1", lambda E: E.dma_start(
        out=xpef.rearrange("p (f s) -> p f s", s=512), in_=pfm_g0[:, :, 0:512],
        accum_op=ALU.add),
         writes=[Rxpef])
    dmaw("dsl0", lambda E: E.dma_start(
        out=xpet.rearrange("p (g d) -> p g d", d=1024), in_=xtm_g0[:, 0:4, :]),
         writes=[Rxpet])
    dmaw("dsl1", lambda E: E.dma_start(
        out=xpet.rearrange("p (g d) -> p g d", d=1024), in_=ptm_g0[:, 0:4, :],
        accum_op=ALU.add),
         writes=[Rxpet])
    P.raw_wait("gpsimd", {"dpf1": P.cnt["dpf1"], "dsl1": P.cnt["dsl1"]})

    conv_group_load(0)
    conv_group_load(1)
    gw_src = gate_w.rearrange("(k p) c -> p k c", p=128)
    w1_src = w1f.rearrange("(k p) c -> p k c", p=128)
    l2_src = lin2_w.rearrange("(k p) c -> p k c", p=128)
    w2_src = w2f.rearrange("(k p) (e c) -> p e k c", p=128, c=512)
    gwb_src = gw_bf.rearrange("(k p) c -> p k c", p=128)
    owb_src = ow_bf.rearrange("(k p) c -> p k c", p=128)
    w1b_src = w1_bf.rearrange("(k p) c -> p k c", p=128)
    l2b_src = l2_bf.rearrange("(k p) c -> p k c", p=128)
    w2b_src = w2_bf.rearrange("(k p) (e c) -> p e k c", p=128, c=512)

    def wb_store(sbuf_view, dst_view, slab_reg, scr_reg, sem="dsb"):
        P.emit("sync", lambda E, s=sbuf_view, d=dst_view: E.dma_start(out=d, in_=s),
               reads=[slab_reg], writes=[scr_reg], inc=sem, k=16)

    dmaw("dgw1", lambda E: E.dma_start(out=gw1a.rearrange("p (k c) -> p k c", c=1024),
                               in_=gw_src),
         writes=[Rgw1])
    wb_store(gw1a.rearrange("p (k c) -> p k c", c=1024), gwb_src, Rgw1, Rsgw)
    ow_src = out_w.rearrange("(k p) c -> p k c", p=128)
    dmaw("doutw", lambda E: E.dma_start(out=outwa.rearrange("p (k c) -> p k c", c=1024),
                                in_=ow_src),
         writes=[Routw])
    wb_store(outwa.rearrange("p (k c) -> p k c", c=1024), owb_src, Routw, Rsow)
    for r in (Rsgw, Rsow):
        r.writers = {"dsb": P.cnt["dsb"]}
    Rgw1.readers["dsb"] = P.cnt["dsb"]
    Routw.readers["dsb"] = P.cnt["dsb"]


    def ln_stage(sc, out_kind, t, bank_ids, scratch_reg):
        sc_base = sc * 512
        bsy, bsq, bb0, bb1 = bank_ids
        if out_kind == "ln1":
            srct, srcreg = xy, (lambda fc: Rxyc[fc])
        else:
            srct, srcreg = ynbf, (lambda fc: Rynbfc[sc][fc])
        for fc in range(NCH):
            xsl = srct[:, fc * 1024 + sc_base: fc * 1024 + sc_base + 512]
            pl = ah[:, PL0 + (fc % 2) * 512: PL0 + (fc % 2) * 512 + 512]
            srg = scratch_reg(fc)
            sx(lambda E, i=xsl, o=pl: E.activation(o, i, AF.Square),
               reads=[srcreg(fc)], writes=[srg], inc="S")
            tx(lambda E, r=xsl, fc=fc: E.matmul(pb[bsy][0:1, :], ones_b[:, 0:1], r,
                                                start=(fc == 0), stop=(fc == 7)),
               reads=[srcreg(fc), Rconst], writes=[Rb[bsy]], inc="T")
            tx(lambda E, r=pl, fc=fc: E.matmul(pb[bsq][0:1, :], ones_b[:, 0:1], r,
                                               start=(fc == 0), stop=(fc == 7)),
               reads=[srg, Rconst], writes=[Rb[bsq]], inc="T")
        mean = statv[0:1, 0:512]
        var = statv[0:1, 512:1024]
        meanb = statv.bitcast(BF16)[0:1, 2048:2560]
        invb = statv.bitcast(BF16)[0:1, 2560:3072]
        vx(lambda E: E.tensor_scalar(meanb, pb[bsy][0:1, :], 1.0 / 1024, None, ALU.mult),
           reads=[Rb[bsy]], writes=[Rstat], inc="V")
        tx(lambda E: E.matmul(pb[bb1][:], ones_b[0:1, :], meanb, start=True, stop=True),
           reads=[Rstat, Rconst], writes=[Rb[bb1]], inc="T")
        vx(lambda E: E.tensor_mul(var, meanb, meanb), reads=[Rstat], writes=[Rstat],
           inc="V")
        vx(lambda E: E.scalar_tensor_tensor(var, pb[bsq][0:1, :], 1.0 / 1024, var,
                                            ALU.mult, ALU.subtract),
           reads=[Rb[bsq], Rstat], writes=[Rstat], inc="V")
        sx(lambda E: E.activation(var, var, AF.Ln, bias=epsc[0:1, :]),
           reads=[Rstat, Rconst], writes=[Rstat], inc="S")
        sx(lambda E: E.activation(invb, var, AF.Exp, scale=-0.5),
           reads=[Rstat], writes=[Rstat], inc="S")
        tx(lambda E: E.matmul(pb[bb0][:], ones_b[0:1, :], invb, start=True, stop=True),
           reads=[Rstat, Rconst], writes=[Rb[bb0]], inc="T")
        for fc in range(NCH):
            xsl = srct[:, fc * 1024 + sc_base: fc * 1024 + sc_base + 512]
            q1 = scrf[:, 0:512]
            vx(lambda E, i=xsl, o=q1: E.tensor_sub(o, i, pb[bb1][:]),
               reads=[srcreg(fc), Rb[bb1]], writes=[Rscrf], inc="V")
            if out_kind == "ln1":
                osl = ynbf[:, fc * 1024 + sc_base: fc * 1024 + sc_base + 512]
                vx(lambda E, i=q1, o=osl: E.tensor_mul(o, i, pb[bb0][:]),
                   reads=[Rscrf, Rb[bb0]], writes=[Rynbfc[sc][fc]], inc="V")
            else:
                slot = 0
                osl = outst[:, 0:512]
                vx(lambda E, i=q1, o=osl: E.tensor_mul(o, i, pb[bb0][:]),
                   reads=[Rscrf, Rb[bb0]], writes=[Rout[slot]], inc="V")
                dst = out_fm[fc * 128:(fc + 1) * 128,
                             t * 1024 + sc_base: t * 1024 + sc_base + 512]
                P.emit("sync", lambda E, s=osl, d=dst: E.dma_start(out=d, in_=s),
                       reads=[Rout[slot]], inc="dout%d" % slot, k=16)

    def router_part1(sc):
        scb = sc * 512
        bank_lg, bank_et = 2, 7
        for fc in range(NCH):
            lhs = rw_s[:, fc * 8: fc * 8 + 8]
            rhs = ynbf[:, fc * 1024 + scb: fc * 1024 + scb + 512]
            tx(lambda E, l=lhs, r=rhs, fc=fc: E.matmul(
                pb[bank_lg][0:8, :], l, r, start=(fc == 0), stop=(fc == 7)),
               reads=[Rrw, Rynbfc[sc][fc]], writes=[Rb[bank_lg]],
               inc=("T" if fc == 7 else None))
        sx(lambda E: E.activation(ebf[0:8, :], pb[bank_lg][0:8, :], AF.Exp,
                                  bias=rb_s[:]),
           reads=[Rb[bank_lg], Rrb], writes=[Rebf], inc="S")
        for u in range(4):
            tin = ebf[0:8, u * 128: u * 128 + 128]
            out = pb[bank_et][:, u * 8: u * 8 + 8]
            tx(lambda E, i=tin, o=out, u=u: E.matmul(
                o, i, id_f[0:8, 0:8], start=(u == 0), stop=(u == 3),
                is_transpose=True),
               reads=[Rebf, Rconst], writes=[Rb[bank_et]],
               inc=("T" if u == 3 else None))
        et_s = scrf[:, 576:608]
        vx(lambda E: E.tensor_copy(et_s, pb[bank_et][:, 0:32]),
           reads=[Rb[bank_et]], writes=[Rscrf], inc="V")
        et = et_s.rearrange("p (a b) -> p a b", b=8)
        zt = scr4[:, 8:12]
        rzt = scr4[:, 12:16]
        m1 = scr4[:, 0:4]
        m2 = scr4[:, 4:8]
        eq = scrf[:, 512:544]
        dd = scrf[:, 544:576]
        cts = ctm[:, sc * 32: sc * 32 + 32]
        vx(lambda E: E.reduce_sum(zt, et, axis=AX.X), reads=[Rscrf],
           writes=[Rscr4], inc="V")
        vx(lambda E: E.reciprocal(rzt, zt), reads=[Rscr4], writes=[Rscr4], inc="V")
        vx(lambda E: E.reduce_max(m1, et, axis=AX.X), reads=[Rscrf],
           writes=[Rscr4], inc="V")
        m1b = m1.rearrange("p (a b) -> p a b", b=1).broadcast_to((128, 4, 8))
        vx(lambda E: E.tensor_tensor(eq.rearrange("p (a b) -> p a b", b=8), et, m1b,
                                     ALU.is_equal),
           reads=[Rscrf, Rscr4], writes=[Rscrf], inc="V")
        vx(lambda E: E.scalar_tensor_tensor(dd, eq, -1e9, et.rearrange("p a b -> p (a b)"),
                                            ALU.mult, ALU.add),
           reads=[Rscrf], writes=[Rscrf], inc="V")
        vx(lambda E: E.reduce_max(m2, dd.rearrange("p (a b) -> p a b", b=8), axis=AX.X),
           reads=[Rscrf], writes=[Rscr4], inc="V")
        m2b = m2.rearrange("p (a b) -> p a b", b=1).broadcast_to((128, 4, 8))
        vx(lambda E: E.tensor_tensor(eq.rearrange("p (a b) -> p a b", b=8), et, m2b,
                                     ALU.is_ge),
           reads=[Rscrf, Rscr4], writes=[Rscrf], inc="V")
        vx(lambda E: E.tensor_tensor(dd.rearrange("p (a b) -> p a b", b=8),
                                     eq.rearrange("p (a b) -> p a b", b=8), et,
                                     ALU.mult),
           reads=[Rscrf], writes=[Rscrf], inc="V")
        rztb = rzt.rearrange("p (a b) -> p a b", b=1).broadcast_to((128, 4, 8))
        vx(lambda E: E.tensor_tensor(cts.rearrange("p (a b) -> p a b", b=8),
                                     dd.rearrange("p (a b) -> p a b", b=8), rztb,
                                     ALU.mult),
           reads=[Rscrf, Rscr4], writes=[Rctm2[sc]], inc="V")

    def router_part2(sc):
        bank_cf = 4
        for u in range(4):
            tin = ctm[:, sc * 32 + u * 8: sc * 32 + u * 8 + 8]
            out = pb[bank_cf][0:8, u * 128: u * 128 + 128]
            tx(lambda E, i=tin, o=out, u=u: E.matmul(
                o, i, id_f[:], start=(u == 0), stop=(u == 3),
                is_transpose=True),
               reads=[Rctm2[sc], Rconst], writes=[Rb[bank_cf]],
               inc=("T" if u == 3 else None))
        sx(lambda E: E.copy(cfm[0:8, :], pb[bank_cf][0:8, 0:512]),
           reads=[Rb[bank_cf]], writes=[Rcfm], inc="S")

    for t in range(NT):
        tb0 = t * 1024
        if t > 0:
            alias_edge(Rpl + Rpg + Rrz + [Rloc, Rglo] + Rmixc, [Rah])
        xfm_g = x_fm.rearrange("(f p) s -> p f s", p=128)
        pfm_g = pos_fm.rearrange("(f p) s -> p f s", p=128)
        xtm_g = x_tm.rearrange("(g p) d -> p g d", p=128)
        ptm_g = pos_tm.rearrange("(g p) d -> p g d", p=128)
        def emit_A(tt_, sc_):
            base = tt_ * 1024 + sc_ * 512
            if tt_ >= NT:
                return
            dmaw("dpf0", lambda E, s=xfm_g[:, :, base: base + 512],
                 d=xpef.rearrange("p (f s) -> p f s", s=512): E.dma_start(
                     out=d, in_=s),
                 writes=[Rxpef])
            dmaw("dpf1", lambda E, s=pfm_g[:, :, base: base + 512],
                 d=xpef.rearrange("p (f s) -> p f s", s=512): E.dma_start(
                     out=d, in_=s, accum_op=ALU.add),
                 writes=[Rxpef])
            g0 = base // 128
            dmaw("dsl0", lambda E, s=xtm_g[:, g0:g0 + 4, :],
                 d=xpet.rearrange("p (g d) -> p g d", d=1024): E.dma_start(
                     out=d, in_=s),
                 writes=[Rxpet])
            dmaw("dsl1", lambda E, s=ptm_g[:, g0:g0 + 4, :],
                 d=xpet.rearrange("p (g d) -> p g d", d=1024): E.dma_start(
                     out=d, in_=s, accum_op=ALU.add),
                 writes=[Rxpet])

        for sc in range(2):
            scb = sc * 512
            if sc == 0:
                for fcx in range(NCH):
                    srcx = x_fm[fcx * 128:(fcx + 1) * 128, tb0:tb0 + 1024]
                    dstx = xy[:, fcx * 1024: fcx * 1024 + 1024]
                    dmaw("dxy", lambda E, s=srcx, d=dstx: E.dma_start(out=d, in_=s),
                         writes=[Rxyc[fcx]])
                for rch in Rxyc:
                    rch.writers = {"dxy": P.cnt["dxy"]}

            scnt = 0
            pcnt = 0
            for q in range(4):
                qs = q % 2
                for hp in range(2):
                    hA = q * 4 + hp * 2
                    fc = hA // 2
                    pair0 = pcnt
                    def emit_av(hh2, pbase, kc):
                        h = q * 4 + hh2
                        av_bank = 3 + hh2
                        slot = (pair0 + kc * 2 + (hh2 - hp * 2)) % 4
                        pls = ah[:, PL0 + slot * 512: PL0 + slot * 512 + 512]
                        lhs = xpet[:, kc * 1024 + 64 * h:
                                   kc * 1024 + 64 * h + 64]
                        out = pb[av_bank][pbase:pbase + 64, :]
                        tx(lambda E, l=lhs, r=pls, o=out, kc=kc: E.matmul(
                            o, l, r, start=(kc == 0), stop=(kc == 3)),
                           reads=[Rpl[slot], Rxpet], writes=[Rb[av_bank]],
                           inc=("T" if kc == 3 else None))

                    for kc in range(4):
                        for hh2, pbase in ((hp * 2, 0), (hp * 2 + 1, 64)):
                            h = q * 4 + hh2
                            b = scnt % 3
                            scnt += 1
                            lhs = xpef[pbase:pbase + 64,
                                       fc * 512 + kc * 128:
                                       fc * 512 + kc * 128 + 128]
                            rhs = xpef[pbase:pbase + 64,
                                       fc * 512: fc * 512 + 512]
                            tx(lambda E, l=lhs, r=rhs, b=b: E.matmul(
                                pb[b][:], l, r, start=True, stop=True),
                               reads=[Rxpef], writes=[Rb[b]], inc="T")
                            slot = pcnt % 4
                            pcnt += 1
                            pls = ah[:, PL0 + slot * 512: PL0 + slot * 512 + 512]
                            zc = ztm[:, qs * 16 + hh2 * 4 + kc:
                                     qs * 16 + hh2 * 4 + kc + 1]
                            sx(lambda E, b=b, o=pls, z=zc: E.activation(
                                o, pb[b][:], AF.Exp, bias=neg8[:], scale=SCALE,
                                accum_out=z),
                               reads=[Rb[b], Rconst], writes=[Rpl[slot], Rzt[qs]],
                               inc="S")
                        if kc >= 1:
                            emit_av(hp * 2, 0, kc - 1)
                            emit_av(hp * 2 + 1, 64, kc - 1)
                    emit_av(hp * 2, 0, 3)
                    emit_av(hp * 2 + 1, 64, 3)
                vx(lambda E, qs=qs: E.reciprocal(rztm[:, qs * 16: qs * 16 + 16],
                                                 ztm[:, qs * 16: qs * 16 + 16]),
                   reads=[Rzt[qs]], writes=[Rrzt[qs]], inc="V")
                for hh in range(4):
                    h = q * 4 + hh
                    fc, pbase = h // 2, 64 * (h % 2)
                    hbase = (h % 2) * 64
                    for kc in range(4):
                        rc = rztm[:, qs * 16 + hh * 4 + kc:
                                  qs * 16 + hh * 4 + kc + 1].broadcast_to((128, 64))
                        out = pb[7][hbase:hbase + 64, kc * 128:kc * 128 + 128]
                        tx(lambda E, i=rc, o=out, kc=kc: E.matmul(
                            o, i, id_b[:], start=(kc == 0), stop=(kc == 3)),
                           reads=[Rrzt[qs], Rconst], writes=[Rb[7]],
                           inc=("T" if kc == 3 else None))
                    if hh % 2 == 1:
                        vx(lambda E: E.tensor_copy(csb[:], pb[7][:]),
                           reads=[Rb[7]], writes=[Rcsb], inc="V")
                        avA, avB = 3 + hh - 1, 3 + hh
                        lA = ah[0:64, LOC + fc * 512: LOC + fc * 512 + 512]
                        lB = ah[64:128, LOC + fc * 512: LOC + fc * 512 + 512]
                        vx(lambda E, o=lA, ab=avA: E.tensor_mul(
                            o, pb[ab][0:64, :], csb[0:64, :]),
                           reads=[Rb[avA], Rcsb], writes=[Rloc], inc="V")
                        vx(lambda E, o=lB, ab=avB: E.tensor_mul(
                            o, pb[ab][64:128, :], csb[64:128, :]),
                           reads=[Rb[avB], Rcsb], writes=[Rloc], inc="V")

            if sc == 1:
                router_part1(0)

            if t == 0 and sc == 0:
                for j in range(32):
                    kc, ic = j // 8, j % 8
                    g, jj = j // 4, j % 4
                    slot = g % 2
                    if jj == 0 and g >= 2:
                        conv_group_load(g)
                    lhs = xpef[:, ic * 512: ic * 512 + 256].rearrange(
                        "p (t f) -> p t f", f=4)[:, :, kc]
                    for oh in range(2):
                        bank = 3 + oh
                        rhs = w2a[:, slot * 4096 + jj * 1024 + oh * 512:
                                  slot * 4096 + jj * 1024 + oh * 512 + 512]
                        tx(lambda E, l=lhs, r=rhs, b=bank, j=j: E.matmul(
                            pb[b][0:64, :], l, r, start=(j == 0), stop=False),
                           reads=[Rxpef, Rconv[slot]], writes=[Rb[bank]],
                           inc=("T" if (j % 4) == 3 and oh == 1 else None))
                for oh in range(2):
                    bank = 3 + oh
                    rhsb = cbrow[0:1, oh * 512: oh * 512 + 512]
                    tx(lambda E, r=rhsb, b=bank: E.matmul(
                        pb[b][0:64, :], ones_b[0:1, 0:64], r, start=False, stop=True),
                       reads=[Rcbrow, Rconst], writes=[Rb[bank]], inc="T")
                    vx(lambda E, b=bank, oh=oh: E.tensor_copy(
                        gt_tm[0:64, oh * 512: oh * 512 + 512], pb[b][0:64, :]),
                       reads=[Rb[bank]], writes=[Rgt_tm], inc="V")
                for fc in range(NCH):
                    bank = 5
                    tin = gt_tm[:, fc * 128: fc * 128 + 128]
                    tx(lambda E, i=tin, b=bank: E.matmul(
                        pbb[b][:, 0:128], i, id_b[:], start=True, stop=True,
                        is_transpose=True),
                       reads=[Rgt_tm, Rconst], writes=[Rb[bank]], inc="T")
                    vx(lambda E, fc=fc, b=bank: E.tensor_copy(
                        gt_fm[:, fc * 128: fc * 128 + 128], pbb[b][:, 0:128]),
                       reads=[Rb[bank]], writes=[Rgt_fm], inc="V")

            scnt = 0
            for q in range(4):
                qs = q % 2
                for hh in range(4):
                    h = q * 4 + hh
                    fc, pbase = h // 2, 64 * (h % 2)
                    b = scnt % 2
                    scnt += 1
                    lhs = gt_fm[pbase:pbase + 64, fc * 128: fc * 128 + 128]
                    rhs = xpef[pbase:pbase + 64, fc * 512: fc * 512 + 512]
                    tx(lambda E, l=lhs, r=rhs, b=b: E.matmul(
                        pb[b][:], l, r, start=True, stop=True),
                       reads=[Rgt_fm, Rxpef], writes=[Rb[b]], inc="T")
                    slot = h % 2
                    pgs = ah[:, PG0 + slot * 512: PG0 + slot * 512 + 512]
                    sx(lambda E, b=b, o=pgs: E.activation(o, pb[b][:], AF.Exp,
                                                          bias=neg8[:], scale=SCALE),
                       reads=[Rb[b], Rconst], writes=[Rpg[slot]], inc="S")
                    for tt in range(4):
                        zout = pb[2][:, qs * 16 + hh * 4 + tt: qs * 16 + hh * 4 + tt + 1]
                        tx(lambda E, l=pgs, o=zout, tt=tt, hh=hh, qs=qs: E.matmul(
                            o, l[:, tt * 128:tt * 128 + 128], ones_b[:, 0:1],
                            start=(hh == 0 and tt == 0), stop=(hh == 3 and tt == 3)),
                           reads=[Rpg[slot], Rconst], writes=[Rb[2]],
                           inc=("T" if (hh == 3 and tt == 3) else None))
                    av_bank = 3 + hh
                    lhs2 = gt_tm[:, h * 64: h * 64 + 64]
                    out = pb[av_bank][pbase:pbase + 64, :]
                    tx(lambda E, l=lhs2, r=pgs, o=out: E.matmul(
                        o, l, r, start=True, stop=True),
                       reads=[Rpg[slot], Rgt_tm], writes=[Rb[av_bank]], inc="T")
                vx(lambda E, qs=qs: E.reciprocal(rztm[:, qs * 16: qs * 16 + 16],
                                                 pb[2][:, qs * 16: qs * 16 + 16]),
                   reads=[Rb[2]], writes=[Rrzt[qs]], inc="V")
                for hh in range(4):
                    h = q * 4 + hh
                    fc, pbase = h // 2, 64 * (h % 2)
                    hbase = (h % 2) * 64
                    for kc in range(4):
                        rc = rztm[:, qs * 16 + hh * 4 + kc:
                                  qs * 16 + hh * 4 + kc + 1].broadcast_to((128, 64))
                        out = pb[7][hbase:hbase + 64, kc * 128:kc * 128 + 128]
                        tx(lambda E, i=rc, o=out, kc=kc: E.matmul(
                            o, i, id_b[:], start=(kc == 0), stop=(kc == 3)),
                           reads=[Rrzt[qs], Rconst], writes=[Rb[7]],
                           inc=("T" if kc == 3 else None))
                    if hh % 2 == 1:
                        vx(lambda E: E.tensor_copy(csb[:], pb[7][:]),
                           reads=[Rb[7]], writes=[Rcsb], inc="V")
                        avA, avB = 3 + hh - 1, 3 + hh
                        gA = ah[0:64, GLO + fc * 512: GLO + fc * 512 + 512]
                        gB = ah[64:128, GLO + fc * 512: GLO + fc * 512 + 512]
                        vx(lambda E, o=gA, ab=avA: E.tensor_mul(
                            o, pb[ab][0:64, :], csb[0:64, :]),
                           reads=[Rb[avA], Rcsb], writes=[Rglo], inc="V")
                        vx(lambda E, o=gB, ab=avB: E.tensor_mul(
                            o, pb[ab][64:128, :], csb[64:128, :]),
                           reads=[Rb[avB], Rcsb], writes=[Rglo], inc="V")

            emit_A(t + (sc == 1), (sc + 1) % 2)

            for m in range(NCH):
                bank = m % 3
                for k in range(16):
                    lhs = gw1a[:, k * 1024 + m * 128: k * 1024 + m * 128 + 128]
                    base = LOC if k < 8 else GLO
                    rr = Rloc if k < 8 else Rglo
                    rhs = ah[:, base + (k % 8) * 512: base + (k % 8) * 512 + 512]
                    tx(lambda E, l=lhs, r=rhs, b=bank, k=k: E.matmul(
                        pb[b][:], l, r, start=(k == 0), stop=(k == 15)),
                       reads=[Rgw1, rr], writes=[Rb[bank]],
                       inc=("T" if k == 15 else None))
                msl = ah[:, MIX + m * 512: MIX + m * 512 + 512]
                sx(lambda E, b=bank, o=msl, m=m: E.activation(
                    o, pb[b][:], AF.Sigmoid, bias=gb_s[:, m:m + 1]),
                   reads=[Rb[bank], Rgb], writes=[Rmixc[m]], inc="S")
                lsl = ah[:, LOC + m * 512: LOC + m * 512 + 512]
                gsl = ah[:, GLO + m * 512: GLO + m * 512 + 512]
                t1 = ah[:, PL0 + (m % 4) * 512: PL0 + (m % 4) * 512 + 512]
                vx(lambda E, a=lsl, b=gsl, o=t1: E.tensor_sub(o, a, b),
                   reads=[Rloc, Rglo], writes=[Rpl[m % 4]], inc="V")
                vx(lambda E, a=t1, b=msl, o=t1: E.tensor_mul(o, a, b),
                   reads=[Rpl[m % 4], Rmixc[m]], writes=[Rpl[m % 4]], inc="V")
                vx(lambda E, a=t1, b=gsl, o=msl: E.tensor_add(o, a, b),
                   reads=[Rpl[m % 4], Rglo], writes=[Rmixc[m]], inc="V")
            for m in range(NCH):
                bank = 3 + (m % 3)
                for k in range(NCH):
                    lhs = outwa[:, k * 1024 + m * 128: k * 1024 + m * 128 + 128]
                    rhs = ah[:, MIX + k * 512: MIX + k * 512 + 512]
                    tx(lambda E, l=lhs, r=rhs, b=bank, k=k: E.matmul(
                        pb[b][:], l, r, start=(k == 0), stop=(k == 7)),
                       reads=[Routw, Rmixc[k]], writes=[Rb[bank]],
                       inc=("T" if k == 7 else None))
                xsl = xy[:, m * 1024 + scb: m * 1024 + scb + 512]
                vx(lambda E, b=bank, o=xsl, m=m: E.scalar_tensor_tensor(
                    o, pb[b][:], ob_s[:, m:m + 1], o, ALU.add, ALU.add),
                   reads=[Rb[bank], Rob, Rxyc[m]], writes=[Rxyc[m]], inc="V")
            ln_stage(sc, "ln1", t, (0, 1, 6, 7), lambda fc: Rpl[fc % 2])

        alias_edge([Rah], Rpl + Rpg + Rrz + [Rloc, Rglo] + Rmixc)
        if t == 0:
            dmaw("dgw1", lambda E: E.dma_start(
                out=gw1a.rearrange("p (k c) -> p k c", c=2048), in_=w1_src),
                 writes=[Rgw1])
            wb_store(gw1a.rearrange("p (k c) -> p k c", c=2048), w1b_src, Rgw1, Rsw1)
        else:
            P.emit("gpsimd", lambda E: E.dma_start(
                out=gw1a.rearrange("p (k c) -> p k c", c=2048), in_=w1b_src),
                reads=[Rsw1], writes=[Rgw1], inc="dgw1", k=16)
        if t == 0:
            alias_edge(Rw2, Rconv)

        for sc in range(2):
            scb = sc * 512
            router_part2(sc)

            for er in range(16):
                e = er // 2
                mslot = 0
                msb = mwsb[:, 0:512]
                if er % 2 == 0:
                    mwb_bank = 5 + mslot
                    lhs = basis[0:8, e * 128: e * 128 + 128]
                    tx(lambda E, l=lhs, b=mwb_bank: E.matmul(pb[b][:], l, cfm[0:8, :],
                                                             start=True, stop=True),
                       reads=[Rcfm, Rconst], writes=[Rb[mwb_bank]], inc="T")
                    sx(lambda E, b=mwb_bank, o=msb: E.copy(o, pb[b][:]),
                       reads=[Rb[mwb_bank]], writes=[Rmwsb[mslot]], inc="S")
                bank = er % 3
                for k in range(NCH):
                    lhs = gw1a[:, k * 2048 + er * 128: k * 2048 + er * 128 + 128]
                    rhs = ynbf[:, k * 1024 + scb: k * 1024 + scb + 512]
                    tx(lambda E, l=lhs, r=rhs, b=bank, k=k: E.matmul(
                        pb[b][:], l, r, start=(k == 0), stop=(k == 7)),
                       reads=[Rgw1, Rynbfc[sc][k]], writes=[Rb[bank]],
                       inc=("T" if k == 7 else None))
                tsl = tbuf[:, er * 512: er * 512 + 512]
                vx(lambda E, b=bank, o=tsl, m=msb: E.tensor_mul(o, pb[b][:], m),
                   reads=[Rb[bank], Rmwsb[mslot]], writes=[Rtb], inc="V")

            for e8 in range(8):
                reg = e8 % 2
                dst = w2a[:, reg * 8192: reg * 8192 + 8192].rearrange(
                    "p (k c) -> p k c", c=512)
                if t == 0 and sc == 0:
                    dmaw("dw2a" if reg == 0 else "dw2b",
                         lambda E, d=dst, e8=e8: E.dma_start(out=d, in_=w2_src[:, e8]),
                         writes=[Rw2[reg]])
                    wb_store(dst, w2b_src[:, e8], Rw2[reg], Rsw2,
                             "dsw2" if reg == 0 else "dsw2b")
                else:
                    P.emit("gpsimd",
                           lambda E, d=dst, e8=e8: E.dma_start(out=d, in_=w2b_src[:, e8]),
                           reads=[Rsw2], writes=[Rw2[reg]],
                           inc=("dw2a" if reg == 0 else "dw2b"), k=16)
                for f4 in range(4):
                    f = e8 * 4 + f4
                    bank = f % 3
                    for k in range(16):
                        lhs = w2a[:, reg * 8192 + k * 512 + f4 * 128:
                                  reg * 8192 + k * 512 + f4 * 128 + 128]
                        rhs = tbuf[:, k * 512: k * 512 + 512]
                        tx(lambda E, l=lhs, r=rhs, b=bank, k=k: E.matmul(
                            pb[b][:], l, r, start=(k == 0), stop=(k == 15)),
                           reads=[Rw2[reg], Rtb], writes=[Rb[bank]],
                           inc=("T" if k == 15 else None))
                    hsl = ah[:, f * 512: f * 512 + 512]
                    sx(lambda E, b=bank, o=hsl: E.activation(o, pb[b][:], AF.Gelu),
                       reads=[Rb[bank]], writes=[Rah], inc="S")

            if t == 0 and sc == 0:
                Rsw2.writers = {"dsw2": P.cnt["dsw2"], "dsw2b": P.cnt["dsw2b"]}
            if sc == 0:
                router_part1(1)
            if t == 0 and sc == 0:
                alias_edge([Rl2[0]], [Rcbrow])
            for kf in range(32):
                slot = kf % 4
                dst = l2a[:, slot * 1024: slot * 1024 + 1024].rearrange(
                    "p (k c) -> p k c", c=1024)
                if t == 0 and sc == 0:
                    dmaw(L2SEM[slot],
                         lambda E, d=dst, kf=kf: E.dma_start(
                             out=d, in_=l2_src[:, kf:kf + 1, :]),
                         writes=[Rl2[slot]])
                    wb_store(dst, l2b_src[:, kf:kf + 1, :], Rl2[slot], Rsl2,
                             "dsl2s" if slot % 2 == 0 else "dsl2t")
                else:
                    P.emit("sync",
                           lambda E, d=dst, kf=kf: E.dma_start(
                               out=d, in_=l2b_src[:, kf:kf + 1, :]),
                           reads=[Rsl2], writes=[Rl2[slot]],
                           inc=L2SEM[slot], k=16)
                for m in range(NCH):
                    lhs = l2a[:, slot * 1024 + m * 128:
                              slot * 1024 + m * 128 + 128]
                    rhs = ah[:, kf * 512: kf * 512 + 512]
                    tx(lambda E, l=lhs, r=rhs, b=m, kf=kf: E.matmul(
                        pb[b][:], l, r, start=(kf == 0), stop=(kf == 31)),
                       reads=[Rl2[slot], Rah], writes=[Rb[m]],
                       inc=("T" if m == NCH - 1 else None))
            if t == 0 and sc == 0:
                Rsl2.writers = {"dsl2s": P.cnt["dsl2s"], "dsl2t": P.cnt["dsl2t"]}
            for m in range(NCH):
                ysl = ynbf[:, m * 1024 + scb: m * 1024 + scb + 512]
                vx(lambda E, b=m, y=ysl, m=m: E.scalar_tensor_tensor(
                    y, pb[b][:], lb_s[:, m:m + 1], y, ALU.add, ALU.add),
                   reads=[Rb[m], Rlb, Rynbfc[sc][m]], writes=[Rynbfc[sc][m]],
                   inc="V")
            ln_stage(sc, "ln2", t, (0, 1, 6, 7), lambda fc: Rah)

        if t + 1 < NT:
            P.emit("gpsimd", lambda E: E.dma_start(
                out=gw1a.rearrange("p (k c) -> p k c", c=1024), in_=gwb_src),
                reads=[Rsgw], writes=[Rgw1], inc="dgw1", k=16)
            P.emit("gpsimd", lambda E: E.dma_start(
                out=outwa.rearrange("p (k c) -> p k c", c=1024), in_=owb_src),
                reads=[Rsow], writes=[Routw], inc="doutw", k=16)

    P.raw_wait("sync", {"dout0": P.cnt["dout0"], "dout1": P.cnt["dout1"]})

    semh = {}
    for name in ("T", "V", "S", "G") + DMA_SEMS:
        semh[name] = stack.enter_context(nc.semaphore(name))

    ctx_lp = nc.allow_low_precision(
        reason="bf16 operands for matmuls are intended; fp32 PSUM accumulation")
    ctx_lp.__enter__()
    with nc.Block() as block:
        def run_queue(qname):
            def runner(E):
                for item in P.q[qname]:
                    if item[0] == "w":
                        E.wait_ge(semh[item[1]], item[2])
                    else:
                        _, fn, inc, k = item
                        ins = fn(E)
                        if inc is not None:
                            ins.then_inc(semh[inc], k)
            return runner

        block.tensor(run_queue("tensor"))
        block.vector(run_queue("vector"))
        block.scalar(run_queue("scalar"))
        block.sync(run_queue("sync"))
        block.gpsimd(run_queue("gpsimd"))

    ctx_lp.__exit__(None, None, None)
    stack.close()
    return nc


_NC_CACHE = None


def kernel(L0_attn_x, L0_attn_pos_emb, L0_attn_global_memory, L0_attn_conv_w,
           L0_attn_conv_b, L0_attn_gate_w, L0_attn_gate_b, L0_attn_out_w,
           L0_attn_out_b, L0_ffn_router_w, L0_ffn_router_b, L0_ffn_w1,
           L0_ffn_w2, L0_ffn_lin2_w, L0_ffn_lin2_b):
    global _NC_CACHE
    if _NC_CACHE is None:
        _NC_CACHE = build_nc()
    nc = _NC_CACHE

    f32 = np.float32
    ac = np.ascontiguousarray
    pos = ac(L0_attn_pos_emb[:S], dtype=f32)
    pos_f = ac(pos.T)
    convw = np.concatenate(
        [L0_attn_conv_w.transpose(2, 1, 0).reshape(4096, D),
         L0_attn_conv_b.reshape(1, D)], axis=0)
    convw = ac(convw, dtype=f32)
    gate_w = ac(L0_attn_gate_w, dtype=f32)
    out_w = ac(L0_attn_out_w, dtype=f32)
    w1fh = ac(L0_ffn_w1.transpose(1, 0, 2).reshape(D, 2048), dtype=f32)
    w2fh = ac(L0_ffn_w2.reshape(2048, FF), dtype=f32)
    lin2 = ac(L0_ffn_lin2_w, dtype=f32)
    gb = ac(L0_attn_gate_b.reshape(8, 128).T, dtype=f32)
    ob = ac(L0_attn_out_b.reshape(8, 128).T, dtype=f32)
    lb = ac(L0_ffn_lin2_b.reshape(8, 128).T, dtype=f32)
    rw = ac(L0_ffn_router_w.reshape(8, 128, 8).transpose(1, 0, 2).reshape(128, 64),
            dtype=f32)
    rb = ac(L0_ffn_router_b.reshape(8, 1), dtype=f32)
    gm = ac(L0_attn_global_memory[0], dtype=f32)

    in_maps = []
    for b in range(8):
        xb = ac(L0_attn_x[b], dtype=f32)
        in_maps.append({
            "x_tm": xb, "x_fm": ac(xb.T),
            "pos_tm": pos, "pos_fm": pos_f,
            "gmem": gm, "convw": convw,
            "gate_w": gate_w, "out_w": out_w,
            "w1f": w1fh, "w2f": w2fh, "lin2_w": lin2,
            "gate_b": gb, "out_b": ob, "lin2_b": lb,
            "router_w": rw, "router_b": rb,
        })
    res = run_bass_kernel_spmd(nc, in_maps, core_ids=list(range(8)))
    out = np.empty((8, S, D), np.float32)
    for b in range(8):
        out[b] = res.results[b]["out_fm"].T
    return out
